# version 12
# baseline (speedup 1.0000x reference)
import numpy as np
import ml_dtypes
from contextlib import ExitStack

import concourse.bass as bass
import concourse.mybir as mybir
from concourse.bass_utils import run_bass_kernel_spmd

F32 = mybir.dt.float32
BF16 = mybir.dt.bfloat16
I32 = mybir.dt.int32
ALU = mybir.AluOpType
ACT = mybir.ActivationFunctionType
AX = mybir.AxisListType.X

NCORES = 8
D = 4096
KC = 32
NT = 512
TILES = 2
HCN = 86
SLOTS = 11
STARTS = [0, 11, 22, 33, 44, 55, 66, 76]
CNTS = [11, 11, 11, 11, 11, 11, 10, 10]
HC2RS = {STARTS[r] + s: (r, s) for r in range(8) for s in range(CNTS[r])}
PAIRS = [(0, 1), (2, 3), (4, 5), (6, 7), (8, 9), (10,)]
SLOT_ORDER = []
for pi, pair in enumerate(PAIRS):
    for s in pair:
        for r in range(8):
            if s < CNTS[r]:
                SLOT_ORDER.append((STARTS[r] + s, pi))
assert len(SLOT_ORDER) == HCN


class Sem:
    def __init__(self, nc, ctx, name):
        self.h = ctx.enter_context(nc.semaphore(name))
        self.n = 0

    def inc(self, inst, k=1):
        inst.then_inc(self.h, k)
        self.n += k
        return self.n


def build_nc(phases="full", shared=True, detect_races=True, debug=False):
    nc = bass.Bass("TRN2", target_bir_lowering=False, detect_race_conditions=detect_races)

    x_in = nc.dram_tensor("x", [TILES * NT, D], F32, kind="ExternalInput").ap()
    aw_in = nc.dram_tensor("aw", [512, D], F32, kind="ExternalInput").ap()
    am_in = nc.dram_tensor("am", [512, D], I32, kind="ExternalInput").ap()
    w1p = nc.dram_tensor("w1p", [SLOTS * 4096, 64], I32, kind="ExternalInput").ap()
    s1p = nc.dram_tensor("s1p", [SLOTS * 8192], F32, kind="ExternalInput").ap()
    w3p = nc.dram_tensor("w3p", [SLOTS * 4096, 64], I32, kind="ExternalInput").ap()
    s3p = nc.dram_tensor("s3p", [SLOTS * 8192], F32, kind="ExternalInput").ap()
    w2p = nc.dram_tensor("w2p", [44032, 64], I32, kind="ExternalInput").ap()
    s2p = nc.dram_tensor("s2p", [88064], F32, kind="ExternalInput").ap()
    c_in = nc.dram_tensor("cvec", [1, D], F32, kind="ExternalInput").ap()
    wm_in = nc.dram_tensor("wmod", [D, 3072], F32, kind="ExternalInput").ap()
    out = nc.dram_tensor("out", [TILES * NT, D], F32, kind="ExternalOutput").ap()
    if debug:
        dbg_mod = nc.dram_tensor("dbg_mod", [8, 3072], F32, kind="ExternalOutput").ap()
        dbg_aw = nc.dram_tensor("dbg_aw", [128, KC, 128], BF16, kind="ExternalOutput").ap()
        dbg_w13 = nc.dram_tensor("dbg_w13", [128, KC, 128], BF16, kind="ExternalOutput").ap()
        dbg_w2 = nc.dram_tensor("dbg_w2", [128, HCN, 128], BF16, kind="ExternalOutput").ap()
        dbg_xres = nc.dram_tensor("dbg_xres", [128, KC, NT], BF16, kind="ExternalOutput").ap()
        dbg_h1 = nc.dram_tensor("dbg_h1", [128, KC, NT], BF16, kind="ExternalOutput").ap()
        dbg_x2 = nc.dram_tensor("dbg_x2", [128, KC, NT], BF16, kind="ExternalOutput").ap()
        dbg_h2 = nc.dram_tensor("dbg_h2", [128, KC, NT], BF16, kind="ExternalOutput").ap()
        dbg_g = nc.dram_tensor("dbg_g", [128, 8, NT], BF16, kind="ExternalOutput").ap()

    ag_mod_i = nc.dram_tensor("ag_mod_i", [3072], F32, kind="Internal")
    ag_mod_o = nc.dram_tensor("ag_mod_o", [8, 3072], F32, kind="Internal")
    ag_aw_i = nc.dram_tensor("ag_aw_i", [4, 128, KC, 128], BF16, kind="Internal")
    ag_aw_o = nc.dram_tensor("ag_aw_o", [8, 4, 128, KC, 128], BF16, kind="Internal",
                             addr_space=("Shared" if shared else "Local"))
    ag_w13_i = nc.dram_tensor("ag_w13_i", [SLOTS, 2, 128, KC, 128], BF16, kind="Internal")
    ag_w13_o = []
    for pi, pair in enumerate(PAIRS):
        ag_w13_o.append(nc.dram_tensor(
            f"ag_w13_o{pi}", [8, len(pair), 2, 128, KC, 128], BF16, kind="Internal",
            addr_space=("Shared" if shared else "Local")))
    ag_w2_i = nc.dram_tensor("ag_w2_i", [4, 128, HCN, 128], BF16, kind="Internal")
    ag_w2_o = nc.dram_tensor("ag_w2_o", [8, 4, 128, HCN, 128], BF16, kind="Internal",
                             addr_space=("Shared" if shared else "Local"))
    bar_i = nc.dram_tensor("bar_i", [8], F32, kind="Internal")
    bar_o = nc.dram_tensor("bar_o", [8, 8], F32, kind="Internal")

    identb_c = nc.inline_tensor(np.eye(128, dtype=ml_dtypes.bfloat16), "identb_c")
    identf_c = nc.inline_tensor(np.eye(128, dtype=np.float32), "identf_c")
    ones1_c = nc.inline_tensor(np.ones((1, 128), np.float32), "ones1_c")
    stx_np = np.zeros((128, 2), ml_dtypes.bfloat16); stx_np[:, 0] = 1
    stq_np = np.zeros((128, 2), ml_dtypes.bfloat16); stq_np[:, 1] = 1
    stx_c = nc.inline_tensor(stx_np, "stx_c")
    stq_c = nc.inline_tensor(stq_np, "stq_c")
    bd4_np = np.zeros((128, 4), ml_dtypes.bfloat16)
    for p in range(128):
        bd4_np[p, p // 32] = 1
    bd4_c = nc.inline_tensor(bd4_np, "bd4_c")
    ex4_np = np.zeros((4, 128), ml_dtypes.bfloat16)
    for m in range(128):
        ex4_np[m // 32, m] = 1
    ex4_c = nc.inline_tensor(ex4_np, "ex4_c")

    ctx = ExitStack()
    with ctx:
        sb = lambda *a, **k: ctx.enter_context(nc.sbuf_tensor(*a, **k))
        identb = sb("identb", [128, 128], BF16)
        identf = sb("identf", [128, 128], F32)
        ones1 = sb("ones1", [1, 128], F32)
        stx = sb("stx", [128, 2], BF16)
        stq = sb("stq", [128, 2], BF16)
        bd4 = sb("bd4", [128, 4], BF16)
        ex4 = sb("ex4", [4, 128], BF16)
        ca1 = sb("ca1", [128, KC], F32)
        sa_v = sb("sa_v", [128, KC], F32)
        cb1 = sb("cb1", [128, KC], F32)
        sb_v = sb("sb_v", [128, KC], F32)
        ga_v = sb("ga_v", [128, KC], F32)
        gb_v = sb("gb_v", [128, KC], F32)
        xres = sb("xres", [128, KC, NT], BF16)
        hbuf = sb("hbuf", [128, KC, NT], BF16)
        tbuf = sb("tbuf", [128, 2, NT], F32)
        wsb = sb("wsb", [128, 24576], BF16)

        def awv(b):
            return wsb.ap()[:, b * 4096:(b + 1) * 4096].rearrange("p (K n) -> p K n", K=KC)

        def w1v(b):
            return wsb.ap()[:, 8192 + b * 4096: 8192 + (b + 1) * 4096].rearrange("p (K n) -> p K n", K=KC)

        def w3v(b):
            return wsb.ap()[:, 16384 + b * 4096: 16384 + (b + 1) * 4096].rearrange("p (K n) -> p K n", K=KC)

        def w2v(b):
            return wsb.ap()[:, b * 11008:(b + 1) * 11008].rearrange("p (H d) -> p H d", H=HCN)

        hx = hbuf.ap().rearrange("p c t -> p (c t)").rearrange("p (g k) -> p g k", g=4)
        ost = hbuf.ap().rearrange("p c t -> p (c t)").bitcast(F32)[:, 0:4096].rearrange(
            "p (b g c d) -> p b g c d", b=2, g=4, c=4)

        ps = lambda *a, **k: ctx.enter_context(nc.psum_tensor(*a, **k))
        pS = ps("pS", [128, 512], F32)
        pT = ps("pT", [128, 2, 1024], BF16)

        block = ctx.enter_context(nc.Block())
        sem = lambda name: Sem(nc, ctx, name)

        sCONST = sem("sCONST")
        sXST = sem("sXST")
        sXTP = sem("sXTP")
        sXEV = sem("sXEV")
        sAWL = sem("sAWL")
        sAWC = sem("sAWC")
        sAWM = sem("sAWM")
        sAWE = sem("sAWE")
        sAWSp = [sem("sAWS0"), sem("sAWS1")]
        sPKp = [sem("sPK0"), sem("sPK1")]
        sDQ = sem("sDQ")
        sWTP = sem("sWTP")
        sWEV = sem("sWEV")
        sWSTp = [sem("sWST0"), sem("sWST1")]
        sWMp = [sem(f"sWM{q}") for q in range(4)]
        sMM = sem("sMM")
        sMEV = sem("sMEV")
        sMST = sem("sMST")
        sAG = sem("sAG")
        sMODL = sem("sMODL")
        sMODD = sem("sMODD")

        _sbuf_before_prep = nc.sbuf_bytes_remaining
        with ExitStack() as prep:
            psb = lambda *a, **k: prep.enter_context(nc.sbuf_tensor(*a, **k))
            pk = psb("pk", [128, 2, 2752], I32)
            nibi = psb("nibi", [128, 5504], I32)
            deqb = psb("deqb", [128, 5504], BF16)
            sclf = psb("sclf", [128, 2, 172], F32)
            wmbuf = psb("wmbuf", [128, 4, 512], F32)
            modsb = psb("modsb", [1, 512], F32)
            stage = psb("stage", [128, 2, 1024], BF16)
            maskgf = psb("maskgf", [128, 128], F32)
            maskgb = psb("maskgb", [128, 128], BF16)
            csb = psb("csb", [128, KC], F32)
            silc = psb("silc", [128, KC], F32)
            gt1 = psb("gt1", [128, KC], F32)
            gt2 = psb("gt2", [128, KC], F32)
            bmb = psb("bmb", [4, 128], BF16)
            _arena = _sbuf_before_prep - nc.sbuf_bytes_remaining
            assert _arena <= 77824, f"prep arena {_arena}B overlaps xsq/stt tail"
            papre = prep.enter_context(nc.psum_tensor("papre", [128, 2, 512], F32))
            pmod = prep.enter_context(nc.psum_tensor("pmod", [128, 3, 512], F32))

            amaskv = nibi.ap().bitcast(BF16)[:, 0:4096]
            awrowv = deqb.ap()[:, 0:4096]

            @block.sync
            def _(sync):
                sync.dma_start(out=identb[:], in_=identb_c.ap()).then_inc(sCONST.h, 16); sCONST.n += 16
                sync.dma_start(out=identf[:], in_=identf_c.ap()).then_inc(sCONST.h, 16); sCONST.n += 16
                sync.dma_start(out=ones1[:], in_=ones1_c.ap()).then_inc(sCONST.h, 16); sCONST.n += 16
                sync.dma_start(out=stx[:], in_=stx_c.ap()).then_inc(sCONST.h, 16); sCONST.n += 16
                sync.dma_start(out=stq[:], in_=stq_c.ap()).then_inc(sCONST.h, 16); sCONST.n += 16
                sync.dma_start(out=bd4[:], in_=bd4_c.ap()).then_inc(sCONST.h, 16); sCONST.n += 16
                sync.dma_start(out=ex4[:], in_=ex4_c.ap()).then_inc(sCONST.h, 16); sCONST.n += 16
                sync.dma_start(out=csb[:], in_=c_in.rearrange("o (ch p) -> (o p) ch", p=128), allow_slow_non_contiguous=True).then_inc(sCONST.h, 16); sCONST.n += 16

            @block.gpsimd
            def _(g):
                g.dma_start(
                    out=hx, in_=x_in.rearrange("(T g p) k -> T g p k", T=TILES, g=4, p=128)[0].transpose([1, 0, 2])
                ).then_inc(sXST.h, 16); sXST.n += 16
                for rt in range(4):
                    if rt > 0:
                        g.wait_ge(sAWC.h, rt * 4)
                    g.dma_start(out=awrowv, in_=aw_in[rt * 128:(rt + 1) * 128, :]).then_inc(sAWL.h, 16); sAWL.n += 16
                    g.dma_start(out=amaskv, in_=am_in[rt * 128:(rt + 1) * 128, :]).then_inc(sAWL.h, 16); sAWL.n += 16
                    if rt < 3:
                        g.wait_ge(sAWM.h, (rt + 1) * 2 + (rt + 1) * 4)
                wmt = wm_in.rearrange("(kt p) (cc n) -> cc kt p n", p=128, n=512)
                for i in range(6 * KC):
                    cc, kt = divmod(i, KC)
                    if i >= 4:
                        g.wait_ge(sMM.h, i - 3)
                    g.dma_start(out=wmbuf[:, i % 4, :], in_=wmt[cc, kt]).then_inc(sWMp[i % 4].h, 16); sWMp[i % 4].n += 16
                    if kt == 3 and cc >= 1:
                        g.wait_ge(sMEV.h, cc + 1)
                        g.dma_start(out=ag_mod_i.ap()[(cc - 1) * 512:cc * 512],
                                    in_=modsb.ap()).then_inc(sMST.h, 16); sMST.n += 16
                g.wait_ge(sMEV.h, 7)
                g.dma_start(out=ag_mod_i.ap()[5 * 512:6 * 512],
                            in_=modsb.ap()).then_inc(sMST.h, 16); sMST.n += 16

            @block.scalar
            def _(s):
                s.wait_ge(sCONST.h, sCONST.n)
                s.activation(silc[:], csb[:], ACT.Silu).then_inc(sMEV.h, 1); sMEV.n += 1

            @block.tensor
            def _(t):
                t.wait_ge(sCONST.h, sCONST.n)
                t.wait_ge(sXST.h, 16)
                for kc in range(KC):
                    par = kc % 2
                    if kc >= 2:
                        t.wait_ge(sXEV.h, kc - 1)
                    for tg in range(4):
                        ins = t.transpose(pT.ap()[:, par, tg * 128:(tg + 1) * 128],
                                          hx[:, tg, kc * 128:(kc + 1) * 128], identb[:])
                    ins.then_inc(sXTP.h, 1); sXTP.n += 1

            @block.scalar
            def _(s):
                for kc in range(KC):
                    par = kc % 2
                    s.wait_ge(sXTP.h, kc + 1)
                    s.copy(xres[:, kc, :], pT.ap()[:, par, 0:512]).then_inc(sXEV.h, 1); sXEV.n += 1


            @block.vector
            def _(v):
                for rt in range(4):
                    v.wait_ge(sAWL.h, 32 * (rt + 1))
                    if rt >= 1:
                        v.wait_ge(sAWM.h, (rt - 1) * 6 + 1)
                    v.reduce_sum(maskgf[:], amaskv.rearrange("p (g w) -> p g w", w=32), axis=AX).then_inc(sAWC.h, 1); sAWC.n += 1
                    v.tensor_copy(maskgb[:], maskgf[:]).then_inc(sAWC.h, 1); sAWC.n += 1
                    v.wait_ge(sAWM.h, rt * 6 + 1)
                    v.tensor_scalar(bmb[:], papre.ap()[0:4, rt % 2, 0:128], 0.0, None, ALU.is_gt).then_inc(sAWC.h, 1); sAWC.n += 1
                    v.wait_ge(sAWM.h, rt * 6 + 2)
                    v.tensor_tensor(
                        awrowv.rearrange("p (kb w) -> p kb w", w=32),
                        awrowv.rearrange("p (kb w) -> p kb w", w=32),
                        papre.ap()[:, rt % 2, 128:256].rearrange("p (kb o) -> p kb o", o=1).broadcast_to([128, 128, 32]),
                        ALU.mult,
                    ).then_inc(sAWC.h, 1); sAWC.n += 1

            @block.tensor
            def _(t):
                for rt in range(4):
                    b = rt % 2
                    t.wait_ge(sAWC.h, rt * 4 + 2)
                    if rt >= 2:
                        t.wait_ge(sAWC.h, (rt - 1) * 4 + 4)
                    t.matmul(papre.ap()[0:4, b, 0:128], bd4[:], maskgb[:], start=True, stop=True).then_inc(sAWM.h, 1); sAWM.n += 1
                    t.wait_ge(sAWC.h, rt * 4 + 3)
                    t.matmul(papre.ap()[:, b, 128:256], ex4[:], bmb[:], start=True, stop=True).then_inc(sAWM.h, 1); sAWM.n += 1
                    t.wait_ge(sAWC.h, rt * 4 + 4)
                    if rt == 0:
                        t.wait_ge(sXEV.h, KC)
                    for pk8 in range(4):
                        tp = (rt * 4 + pk8) % 2
                        if rt * 4 + pk8 >= 2:
                            t.wait_ge(sAWE.h, rt * 4 + pk8 - 1)
                        for j in range(8):
                            kt = pk8 * 8 + j
                            ins = t.transpose(pT.ap()[:, tp, j * 128:(j + 1) * 128],
                                              awrowv[:, kt * 128:(kt + 1) * 128], identb[:])
                        ins.then_inc(sAWM.h, 1); sAWM.n += 1

            @block.scalar
            def _(s):
                for i in range(16):
                    rt, pk8 = divmod(i, 4)
                    tp = i % 2
                    s.wait_ge(sAWM.h, rt * 6 + 2 + pk8 + 1)
                    if i >= 2:
                        s.wait_ge(sAWSp[i % 2].h, 16 * (i // 2))
                    s.copy(stage[:, tp, :], pT.ap()[:, tp, :]).then_inc(sAWE.h, 1); sAWE.n += 1

            @block.sync
            def _(sync):
                for i in range(16):
                    rt, pk8 = divmod(i, 4)
                    tp = i % 2
                    sync.wait_ge(sAWE.h, i + 1)
                    sync.dma_start(
                        out=ag_aw_i.ap()[rt, :, pk8 * 8:(pk8 + 1) * 8, :],
                        in_=stage[:, tp, :].rearrange("p (K n) -> p K n", K=8),
                    ).then_inc(sAWSp[i % 2].h, 16); sAWSp[i % 2].n += 16

            dq_tiles = []
            for s in range(SLOTS):
                dq_tiles.append(("w13", 0, s))
                dq_tiles.append(("w13", 1, s))
            for dt in range(4):
                for hf in range(2):
                    dq_tiles.append(("w2", dt, hf))

            w1pr = w1p.rearrange("(s p j) c -> s p (j c)", s=SLOTS, p=128, j=32)
            w3pr = w3p.rearrange("(s p j) c -> s p (j c)", s=SLOTS, p=128, j=32)
            s1pr = s1p.rearrange("(s p g) -> s p g", s=SLOTS, p=128)
            s3pr = s3p.rearrange("(s p g) -> s p g", s=SLOTS, p=128)
            w2pr = w2p.rearrange("(dt p j) c -> dt p (j c)", dt=4, p=128, j=86)
            s2pr = s2p.rearrange("(dt p g) -> dt p g", dt=4, p=128)

            def npacks(kind):
                return 4 if kind == "w13" else 6

            total_packs = sum(npacks(t[0]) for t in dq_tiles)
            pack_at_tile = []
            _pc = 0
            for _kind, _a, _b in dq_tiles:
                _pc += npacks(_kind)
                pack_at_tile.append(_pc)

            def emit_dq_store(sync, i):
                kind, a, bidx = dq_tiles[i]
                base = pack_at_tile[i - 1] if i >= 1 else 0
                nkt = KC if kind == "w13" else 43
                for p8 in range(npacks(kind)):
                    pkc = base + p8
                    tp = pkc % 2
                    sync.wait_ge(sWEV.h, pkc + 1)
                    nk8 = min((p8 + 1) * 8, nkt) - p8 * 8
                    if kind == "w13":
                        o = ag_w13_i.ap()[bidx, a, :, p8 * 8:p8 * 8 + nk8, :]
                    else:
                        o = ag_w2_i.ap()[a, :, bidx * 43 + p8 * 8: bidx * 43 + p8 * 8 + nk8, :]
                    sync.dma_start(out=o, in_=stage[:, tp, 0:nk8 * 128].rearrange("p (K n) -> p K n", K=nk8)).then_inc(sWSTp[pkc % 2].h, 16); sWSTp[pkc % 2].n += 16

            @block.sync
            def _(sync):
                for i, (kind, a, bidx) in enumerate(dq_tiles):
                    par = i % 2
                    if i >= 2:
                        sync.wait_ge(sDQ.h, i - 1)
                    if kind == "w13":
                        src = (w1pr if a == 0 else w3pr)[bidx]
                        sync.dma_start(out=pk[:, par, 0:2048], in_=src).then_inc(sPKp[par].h, 16); sPKp[par].n += 16
                        ssrc = (s1pr if a == 0 else s3pr)[bidx]
                        sync.dma_start(out=sclf[:, par, 0:64], in_=ssrc).then_inc(sPKp[par].h, 16); sPKp[par].n += 16
                    else:
                        sync.dma_start(out=pk[:, par, :], in_=w2pr[a][:, bidx * 2752:(bidx + 1) * 2752]).then_inc(sPKp[par].h, 16); sPKp[par].n += 16
                        sync.dma_start(out=sclf[:, par, 0:86], in_=s2pr[a][:, bidx * 86:(bidx + 1) * 86]).then_inc(sPKp[par].h, 16); sPKp[par].n += 16
                    if i >= 1:
                        emit_dq_store(sync, i - 1)
                emit_dq_store(sync, len(dq_tiles) - 1)

            @block.vector
            def _(v):
                for i, (kind, a, bidx) in enumerate(dq_tiles):
                    par = i % 2
                    if i == 0:
                        v.wait_ge(sAWM.h, 24)
                    if 3 <= i < 14 and (i - 3) % 2 == 0:
                        cc = (i - 3) // 2
                        v.wait_ge(sMM.h, KC * (cc + 1))
                        if cc >= 1:
                            v.wait_ge(sMST.h, 16 * cc)
                        v.tensor_copy(modsb.ap(), pmod.ap()[0:1, cc % 3, :]).then_inc(sMEV.h, 1); sMEV.n += 1
                    v.wait_ge(sPKp[par].h, 32 * (i // 2 + 1))
                    if i >= 1:
                        v.wait_ge(sWTP.h, pack_at_tile[i - 1])
                    if kind == "w13":
                        nv = nibi.ap()[:, 0:4096].rearrange("p (j two c) -> p j two c", two=2, c=64)
                        pkv = pk.ap()[:, par, 0:2048].rearrange("p (j c) -> p j c", c=64)
                        v.tensor_scalar(nv[:, :, 0, :], pkv, 4, None, ALU.arith_shift_right)
                        v.tensor_scalar(nv[:, :, 1, :], pkv, 28, 28, ALU.arith_shift_left, ALU.arith_shift_right)
                        v.tensor_tensor(
                            deqb.ap()[:, 0:4096].rearrange("p (g c) -> p g c", c=64),
                            nibi.ap()[:, 0:4096].rearrange("p (g c) -> p g c", c=64),
                            sclf.ap()[:, par, 0:64].unsqueeze(2).broadcast_to([128, 64, 64]),
                            ALU.mult,
                        ).then_inc(sDQ.h, 1); sDQ.n += 1
                    else:
                        nv = nibi.ap().rearrange("p (j two c) -> p j two c", two=2, c=64)
                        pkv = pk.ap()[:, par, :].rearrange("p (j c) -> p j c", c=64)
                        v.tensor_scalar(nv[:, :, 0, :], pkv, 4, None, ALU.arith_shift_right)
                        v.tensor_scalar(nv[:, :, 1, :], pkv, 28, 28, ALU.arith_shift_left, ALU.arith_shift_right)
                        v.tensor_tensor(
                            deqb.ap().rearrange("p (g c) -> p g c", c=64),
                            nibi.ap().rearrange("p (g c) -> p g c", c=64),
                            sclf.ap()[:, par, 0:86].unsqueeze(2).broadcast_to([128, 86, 64]),
                            ALU.mult,
                        ).then_inc(sDQ.h, 1); sDQ.n += 1

            def emit_mod_group(t, cc):
                if cc == 0:
                    t.wait_ge(sMEV.h, 1)
                for kt in range(KC):
                    i = cc * KC + kt
                    t.wait_ge(sWMp[i % 4].h, 16 * (i // 4 + 1))
                    if kt == 0 and cc >= 3:
                        t.wait_ge(sMST.h, 16 * (cc - 2))
                    t.matmul(pmod.ap()[0:1, cc % 3, :], silc[:, kt:kt + 1], wmbuf[:, i % 4, :],
                             start=(kt == 0), stop=(kt == KC - 1)).then_inc(sMM.h, 1); sMM.n += 1

            MOD_AFTER = {1: 0, 3: 1, 5: 2, 7: 3, 9: 4, 11: 5}

            @block.tensor
            def _(t):
                pkc = 0
                for i, (kind, a, bidx) in enumerate(dq_tiles):
                    t.wait_ge(sDQ.h, i + 1)
                    nkt = KC if kind == "w13" else 43
                    for p8 in range(npacks(kind)):
                        tp = pkc % 2
                        if pkc < 2:
                            t.wait_ge(sAWE.h, 16)
                        else:
                            t.wait_ge(sWEV.h, pkc - 1)
                        js = range(p8 * 8, min((p8 + 1) * 8, nkt))
                        for jj, kt in enumerate(js):
                            ins = t.transpose(pT.ap()[:, tp, jj * 128:(jj + 1) * 128],
                                              deqb.ap()[:, kt * 128:(kt + 1) * 128], identb[:])
                        ins.then_inc(sWTP.h, 1); sWTP.n += 1
                        pkc += 1
                    if i in MOD_AFTER:
                        emit_mod_group(t, MOD_AFTER[i])

            @block.scalar
            def _(s):
                pkc = 0
                for i, (kind, a, bidx) in enumerate(dq_tiles):
                    for p8 in range(npacks(kind)):
                        tp = pkc % 2
                        s.wait_ge(sWTP.h, pkc + 1)
                        if pkc >= 2:
                            s.wait_ge(sWSTp[pkc % 2].h, 16 * (pkc // 2))
                        nkt = KC if kind == "w13" else 43
                        w = (min((p8 + 1) * 8, nkt) - p8 * 8) * 128
                        s.copy(stage[:, tp, 0:w], pT.ap()[:, tp, 0:w]).then_inc(sWEV.h, 1); sWEV.n += 1
                        pkc += 1


            @block.gpsimd
            def _(g):
                rg = [list(range(NCORES))]

                def barrier():
                    g.wait_ge(sAG.h, sAG.n)
                    g.collective_compute("AllGather", ALU.bypass, replica_groups=rg,
                                         ins=[bar_i.ap().opt()], outs=[bar_o.ap().opt()]).then_inc(sAG.h, 1); sAG.n += 1

                g.wait_ge(sMST.h, 96)
                g.collective_compute("AllGather", ALU.bypass, replica_groups=rg,
                                     ins=[ag_mod_i.ap().opt()], outs=[ag_mod_o.ap().opt()]).then_inc(sAG.h, 1); sAG.n += 1
                g.wait_ge(sAWSp[0].h, sAWSp[0].n)
                g.wait_ge(sAWSp[1].h, sAWSp[1].n)
                g.wait_ge(sAG.h, 1)
                g.collective_compute("AllGather", ALU.bypass, replica_groups=rg,
                                     ins=[ag_aw_i.ap().opt()], outs=[ag_aw_o.ap().opt()]).then_inc(sAG.h, 1); sAG.n += 1
                barrier()
                for pi, pair in enumerate(PAIRS):
                    last_slot = pair[-1]
                    tile_idx = 2 * last_slot + 1
                    np_tot = pack_at_tile[tile_idx]
                    g.wait_ge(sWSTp[0].h, 16 * ((np_tot + 1) // 2))
                    g.wait_ge(sWSTp[1].h, 16 * (np_tot // 2))
                    g.wait_ge(sAG.h, sAG.n)
                    g.collective_compute("AllGather", ALU.bypass, replica_groups=rg,
                                         ins=[ag_w13_i.ap()[pair[0]:last_slot + 1].opt()],
                                         outs=[ag_w13_o[pi].ap().opt()]).then_inc(sAG.h, 1); sAG.n += 1
                    barrier()
                g.wait_ge(sWSTp[0].h, 16 * ((total_packs + 1) // 2))
                g.wait_ge(sWSTp[1].h, 16 * (total_packs // 2))
                g.wait_ge(sAG.h, sAG.n)
                g.collective_compute("AllGather", ALU.bypass, replica_groups=rg,
                                     ins=[ag_w2_i.ap().opt()], outs=[ag_w2_o.ap().opt()]).then_inc(sAG.h, 1); sAG.n += 1
                barrier()
                if debug:
                    g.wait_ge(sAG.h, 17)
                    g.dma_start(out=dbg_mod, in_=ag_mod_o.ap()).then_inc(sAG.h, 16)
                    g.dma_start(out=dbg_aw, in_=ag_aw_o.ap()[0, 0]).then_inc(sAG.h, 16)
                    g.dma_start(out=dbg_w13, in_=ag_w13_o[0].ap()[0, 0, 0]).then_inc(sAG.h, 16)
                    g.dma_start(out=dbg_w2, in_=ag_w2_o.ap()[0, 0]).then_inc(sAG.h, 16)
                    g.wait_ge(sAG.h, 17 + 64)
                    sAG.n += 64

            @block.sync
            def _(sync):
                sync.wait_ge(sAG.h, 1)
                flat = ag_mod_o.ap().rearrange("r x -> (r x)").rearrange(
                    "(v ch p) -> v p ch", v=6, p=128)
                for vi, dst in enumerate([sa_v, ca1, gt1, sb_v, cb1, gt2]):
                    sync.dma_start(out=dst[:], in_=flat[vi], allow_slow_non_contiguous=True).then_inc(sMODL.h, 16); sMODL.n += 16

            @block.vector
            def _(v):
                v.wait_ge(sMODL.h, 16 * 6)
                v.tensor_scalar(ca1[:], ca1[:], 1.0, None, ALU.add)
                v.tensor_scalar(cb1[:], cb1[:], 1.0, None, ALU.add)
                v.tensor_scalar(ga_v[:], gt1[:], 0.0, None, ALU.is_ge)
                v.tensor_scalar(gb_v[:], gt2[:], 0.0, None, ALU.is_ge)
                v.tensor_scalar(ga_v[:], ga_v[:], 2.0, None, ALU.mult)
                v.tensor_scalar(gb_v[:], gb_v[:], 2.0, None, ALU.mult)
                v.tensor_scalar(ga_v[:], ga_v[:], -1.0, None, ALU.add)
                v.tensor_scalar(gb_v[:], gb_v[:], -1.0, None, ALU.add)
                nc.vector.engine_nop().then_inc(sMODD.h, 1); sMODD.n += 1

            @block.scalar
            def _(s):
                s.wait_ge(sMODD.h, 1)
                s.activation(gt1[:], gt1[:], ACT.Abs)
                s.activation(gt2[:], gt2[:], ACT.Abs).then_inc(sMODD.h, 1); sMODD.n += 1

            @block.vector
            def _(v):
                v.wait_ge(sMODD.h, 2)
                v.tensor_scalar(gt1[:], gt1[:], 0.25, None, ALU.max)
                v.tensor_scalar(gt2[:], gt2[:], 0.25, None, ALU.max)
                v.tensor_tensor(ga_v[:], ga_v[:], gt1[:], ALU.mult)
                v.tensor_tensor(gb_v[:], gb_v[:], gt2[:], ALU.mult).then_inc(sMODD.h, 1); sMODD.n += 1

        if phases == "prep":
            @block.gpsimd
            def _(g):
                g.wait_ge(sAG.h, sAG.n)
            @block.sync
            def _(sync):
                sync.wait_ge(sMODL.h, sMODL.n)
            return nc

        gbuf = sb("gbuf", [128, HCN, NT], BF16)
        gflat = gbuf.ap().rearrange("p h t -> p (h t)")
        stt = gflat.bitcast(F32)[0:1, 19968:22016]
        xsq = gflat[:, 38912:39936].rearrange("p (a t) -> p a t", a=2)
        pA = ps("pA", [128, 2, 512], F32)
        pB = ps("pB", [128, 2, 512], F32)

        prev = {}
        ntiles = 1 if phases == "t0" else TILES
        for T in range(ntiles):
            u = f"t{T}_"
            sSQ = sem(u + "sSQ")
            sSTAT = sem(u + "sSTAT")
            sSS = sem(u + "sSS")
            sBC = sem(u + "sBC")
            sLN = sem(u + "sLN")
            sLNT = sem(u + "sLNT")
            sASTRp = [sem(u + "sASTR0"), sem(u + "sASTR1")]
            sAMM = sem(u + "sAMM")
            sAEP = sem(u + "sAEP")
            sSQ2 = sem(u + "sSQ2")
            sSTAT2 = sem(u + "sSTAT2")
            sSS2 = sem(u + "sSS2")
            sBC2 = sem(u + "sBC2")
            sLN2 = sem(u + "sLN2")
            sLNT2 = sem(u + "sLNT2")
            sWSTRp = [sem(u + "sWSTR0"), sem(u + "sWSTR1")]
            sUV = sem(u + "sUV")
            sSIL = sem(u + "sSIL")
            sGATE = sem(u + "sGATE")
            sW2STRp = [sem(u + "sW2STR0"), sem(u + "sW2STR1")]
            sFMM = sem(u + "sFMM")
            sFEP = sem(u + "sFEP")
            sOTP = sem(u + "sOTP")
            sOEV = sem(u + "sOEV")
            sOSTp = [sem(u + "sOST0"), sem(u + "sOST1")]
            sXST2 = sem(u + "sXST2")
            sXTP2 = sem(u + "sXTP2")
            sXEV2 = sem(u + "sXEV2")

            if T == 1:
                @block.gpsimd
                def _(g, T=T):
                    g.wait_ge(prev["sOSTp"][0].h, prev["sOSTp"][0].n)
                    g.wait_ge(prev["sOSTp"][1].h, prev["sOSTp"][1].n)
                    g.dma_start(
                        out=hx, in_=x_in.rearrange("(T g p) k -> T g p k", T=TILES, g=4, p=128)[T].transpose([1, 0, 2])
                    ).then_inc(sXST2.h, 16)
                sXST2.n += 16

                @block.tensor
                def _(t, T=T):
                    t.wait_ge(sXST2.h, 16)
                    for kc in range(KC):
                        par = kc % 2
                        if kc >= 2:
                            t.wait_ge(sXEV2.h, kc - 1)
                        for tg in range(4):
                            ins = t.transpose(pT.ap()[:, par, tg * 128:(tg + 1) * 128],
                                              hx[:, tg, kc * 128:(kc + 1) * 128], identb[:])
                        ins.then_inc(sXTP2.h, 1)
                sXTP2.n += KC

                @block.scalar
                def _(s, T=T):
                    for kc in range(KC):
                        par = kc % 2
                        s.wait_ge(sXTP2.h, kc + 1)
                        s.copy(xres[:, kc, :], pT.ap()[:, par, 0:512]).then_inc(sXEV2.h, 1)
                sXEV2.n += KC
                sXEVcur, sXEVn = sXEV2, KC
            else:
                sXEVcur, sXEVn = sXEV, KC

            def emit_ln(sSQ_, sSTAT_, sSS_, sBC_, sLN_, sLNT_, scale_v, bias_v,
                        wait_chunk=None, xev=None, xevn=0):
                @block.scalar
                def _(s):
                    for c in range(KC):
                        par = c % 2
                        if xev is not None:
                            s.wait_ge(xev.h, min(c + 1, xevn))
                        if wait_chunk is not None:
                            s.wait_ge(wait_chunk.h, wait_chunk.n - KC + c + 1)
                        if c >= 2:
                            s.wait_ge(sSTAT_.h, c - 1)
                        s.activation(xsq[:, par, :], xres[:, c, :], ACT.Square).then_inc(sSQ_.h, 1)
                sSQ_.n += KC

                @block.tensor
                def _(t):
                    t.wait_ge(sCONST.h, sCONST.n)
                    for c in range(KC):
                        par = c % 2
                        if xev is not None:
                            t.wait_ge(xev.h, min(c + 1, xevn))
                        if wait_chunk is not None:
                            t.wait_ge(wait_chunk.h, wait_chunk.n - KC + c + 1)
                        t.matmul(pS.ap()[0:1, :], stx[:, 0:1], xres[:, c, :], start=(c == 0), stop=(c == KC - 1))
                        t.wait_ge(sSQ_.h, c + 1)
                        ins = t.matmul(pB.ap()[0:1, 0, :], stx[:, 0:1], xsq[:, par, :], start=(c == 0), stop=(c == KC - 1))
                        ins.then_inc(sSTAT_.h, 1)
                sSTAT_.n += KC

                if phases == "stats":
                    @block.gpsimd
                    def _(g):
                        g.wait_ge(sSTAT_.h, KC)
                    return

                @block.vector
                def _(v):
                    v.wait_ge(sSTAT_.h, KC)
                    rstd = stt[:, 0:NT]; nmr = stt[:, NT:2 * NT]
                    mu = stt[:, 2 * NT:3 * NT]; tmp = stt[:, 3 * NT:4 * NT]
                    v.tensor_scalar(mu, pS.ap()[0:1, :], 1.0 / D, None, ALU.mult)
                    v.tensor_scalar(tmp, pB.ap()[0:1, 0, :], 1.0 / D, None, ALU.mult)
                    v.tensor_tensor(rstd, mu, mu, ALU.mult)
                    v.tensor_tensor(tmp, tmp, rstd, ALU.subtract)
                    nc.vector.engine_nop().then_inc(sSS_.h, 1)
                sSS_.n += 1

                @block.scalar
                def _(s):
                    s.wait_ge(sSS_.h, 1)
                    s.activation(stt[:, 3 * NT:4 * NT], stt[:, 3 * NT:4 * NT], ACT.Sqrt, bias=np.float32(1e-5)).then_inc(sSS_.h, 1)
                sSS_.n += 1

                @block.vector
                def _(v):
                    v.wait_ge(sSS_.h, 2)
                    rstd = stt[:, 0:NT]; nmr = stt[:, NT:2 * NT]
                    mu = stt[:, 2 * NT:3 * NT]; tmp = stt[:, 3 * NT:4 * NT]
                    v.reciprocal(rstd, tmp)
                    v.scalar_tensor_tensor(nmr, mu, -1.0, rstd, ALU.mult, ALU.mult)
                    nc.vector.engine_nop().then_inc(sSS_.h, 1)
                sSS_.n += 1

                @block.tensor
                def _(t):
                    t.wait_ge(sSS_.h, 3)
                    t.matmul(pB.ap()[:, 0, :], ones1[:], stt[:, 0:NT], start=True, stop=True)
                    t.matmul(pB.ap()[:, 1, :], ones1[:], stt[:, NT:2 * NT], start=True, stop=True).then_inc(sBC_.h, 1)
                sBC_.n += 1

                if phases == "bcast":
                    @block.gpsimd
                    def _(g):
                        g.wait_ge(sBC_.h, 1)
                    return

                @block.vector
                def _(v):
                    v.wait_ge(sBC_.h, 1)
                    v.wait_ge(sMODD.h, 3)
                    for c in range(KC):
                        par = c % 2
                        if c >= 2:
                            v.wait_ge(sLN_.h, c - 1)
                        v.tensor_tensor(tbuf[:, par, :], xres[:, c, :], pB.ap()[:, 0, :], ALU.mult)
                        v.tensor_tensor(tbuf[:, par, :], tbuf[:, par, :], pB.ap()[:, 1, :], ALU.add).then_inc(sLNT_.h, 1)
                sLNT_.n += KC

                @block.scalar
                def _(s):
                    for c in range(KC):
                        par = c % 2
                        s.wait_ge(sLNT_.h, c + 1)
                        s.activation(hbuf[:, c, :], tbuf[:, par, :], ACT.Identity,
                                     scale=scale_v[:, c:c + 1], bias=bias_v[:, c:c + 1]).then_inc(sLN_.h, 1)
                sLN_.n += KC

            if debug and T == 0:
                sDBG = sem(u + "sDBG")

                @block.sync
                def _(sync):
                    sync.wait_ge(sXEV.h, KC)
                    sync.dma_start(out=dbg_xres, in_=xres[:]).then_inc(sDBG.h, 16)
                sDBG.n += 16

            emit_ln(sSQ, sSTAT, sSS, sBC, sLN, sLNT, ca1, sa_v,
                    wait_chunk=None, xev=sXEVcur, xevn=sXEVn)

            if debug and T == 0:
                @block.sync
                def _(sync):
                    sync.wait_ge(sLN.h, KC)
                    sync.dma_start(out=dbg_h1, in_=hbuf[:]).then_inc(sDBG.h, 16)
                sDBG.n += 16

            if phases in ("stats", "bcast"):
                break

            if phases == "ln1":
                @block.gpsimd
                def _(g):
                    g.wait_ge(sLN.h, KC)
                break

            @block.sync
            def _(sync):
                for NCc in range(KC):
                    b = NCc % 2
                    sync.wait_ge(sAG.h, 3)
                    if NCc >= 2:
                        sync.wait_ge(sAMM.h, NCc - 1)
                    if T == 1 and NCc < 2:
                        sync.wait_ge(prev["sFMM"].h, prev["sFMM"].n)
                    r, ncl = NCc // 4, NCc % 4
                    sync.dma_start(out=awv(b), in_=ag_aw_o.ap()[r, ncl]).then_inc(sASTRp[b].h, 16)
            sASTRp[0].n += 16 * (KC // 2); sASTRp[1].n += 16 * (KC // 2)

            @block.tensor
            def _(t):
                for NCc in range(KC):
                    b = NCc % 2
                    t.wait_ge(sASTRp[b].h, 16 * (NCc // 2 + 1))
                    t.wait_ge(sLN.h, KC)
                    if NCc >= 2:
                        t.wait_ge(sAEP.h, NCc - 1)
                    for k in range(KC):
                        ins = t.matmul(pA.ap()[:, b, :], awv(b)[:, k, :], hbuf[:, k, :],
                                       start=(k == 0), stop=(k == KC - 1))
                    ins.then_inc(sAMM.h, 1)
            sAMM.n += KC

            @block.vector
            def _(v):
                for NCc in range(KC):
                    b = NCc % 2
                    v.wait_ge(sAMM.h, NCc + 1)
                    if debug and T == 0 and NCc == 0:
                        v.wait_ge(sDBG.h, 32)
                    v.scalar_tensor_tensor(xres[:, NCc, :], pA.ap()[:, b, :], ga_v[:, NCc:NCc + 1],
                                           xres[:, NCc, :], ALU.mult, ALU.add).then_inc(sAEP.h, 1)
            sAEP.n += KC

            if phases == "attn":
                @block.gpsimd
                def _(g):
                    g.wait_ge(sAEP.h, KC)
                break

            if debug and T == 0:
                @block.sync
                def _(sync):
                    sync.wait_ge(sAEP.h, KC)
                    sync.dma_start(out=dbg_x2, in_=xres[:]).then_inc(sDBG.h, 16)
                sDBG.n += 16

            emit_ln(sSQ2, sSTAT2, sSS2, sBC2, sLN2, sLNT2, cb1, sb_v,
                    wait_chunk=sAEP, xev=None, xevn=0)

            if debug and T == 0:
                @block.sync
                def _(sync):
                    sync.wait_ge(sLN2.h, KC)
                    sync.dma_start(out=dbg_h2, in_=hbuf[:]).then_inc(sDBG.h, 16)
                sDBG.n += 16

            if phases == "ln2":
                @block.gpsimd
                def _(g):
                    g.wait_ge(sLN2.h, KC)
                break

            @block.sync
            def _(sync):
                for i, (hc, pi) in enumerate(SLOT_ORDER):
                    b = i % 2
                    sync.wait_ge(sAG.h, 5 + 2 * pi)
                    if i >= 2:
                        sync.wait_ge(sUV.h, 2 * (i - 2) + 2)
                    r, s = HC2RS[hc]
                    pair = PAIRS[pi]
                    sub = s - pair[0]
                    sync.dma_start(out=w1v(b), in_=ag_w13_o[pi].ap()[r, sub, 0]).then_inc(sWSTRp[b].h, 16)
                    sync.dma_start(out=w3v(b), in_=ag_w13_o[pi].ap()[r, sub, 1]).then_inc(sWSTRp[b].h, 16)
            sWSTRp[0].n += 32 * (HCN // 2 + HCN % 2); sWSTRp[1].n += 32 * (HCN // 2)

            @block.tensor
            def _(t):
                for i, (hc, pi) in enumerate(SLOT_ORDER):
                    b = i % 2
                    t.wait_ge(sWSTRp[b].h, 32 * (i // 2 + 1))
                    t.wait_ge(sLN2.h, KC)
                    if i >= 2:
                        t.wait_ge(sSIL.h, i - 1)
                        t.wait_ge(sGATE.h, i - 1)
                    for k in range(KC):
                        ins = t.matmul(pA.ap()[:, b, :], w1v(b)[:, k, :], hbuf[:, k, :],
                                       start=(k == 0), stop=(k == KC - 1))
                    ins.then_inc(sUV.h, 1)
                    for k in range(KC):
                        ins = t.matmul(pB.ap()[:, b, :], w3v(b)[:, k, :], hbuf[:, k, :],
                                       start=(k == 0), stop=(k == KC - 1))
                    ins.then_inc(sUV.h, 1)
            sUV.n += 2 * HCN

            @block.scalar
            def _(s):
                for i, (hc, pi) in enumerate(SLOT_ORDER):
                    b = i % 2
                    s.wait_ge(sUV.h, 2 * i + 1)
                    if i >= 2:
                        s.wait_ge(sGATE.h, i - 1)
                    s.activation(tbuf[:, b, :], pA.ap()[:, b, :], ACT.Silu).then_inc(sSIL.h, 1)
            sSIL.n += HCN

            @block.vector
            def _(v):
                for i, (hc, pi) in enumerate(SLOT_ORDER):
                    b = i % 2
                    v.wait_ge(sUV.h, 2 * i + 2)
                    v.wait_ge(sSIL.h, i + 1)
                    v.tensor_tensor(gbuf[:, hc, :], tbuf[:, b, :], pB.ap()[:, b, :], ALU.mult).then_inc(sGATE.h, 1)
            sGATE.n += HCN

            if phases == "uv":
                @block.gpsimd
                def _(g):
                    g.wait_ge(sGATE.h, HCN)
                break

            if debug and T == 0:
                @block.sync
                def _(sync):
                    sync.wait_ge(sGATE.h, HCN)
                    sync.dma_start(out=dbg_g, in_=gbuf[:, 0:8, :]).then_inc(sDBG.h, 16)
                sDBG.n += 16

            outr = out.rearrange("(Tt g p) d -> Tt g p d", Tt=TILES, g=4, p=128)

            def emit_out_store(sync, DS):
                pd = DS % 2
                sync.wait_ge(sOEV.h, 4 * (DS + 1))
                for tg in range(4):
                    sync.dma_start(out=outr[T, tg][:, DS * 512:(DS + 1) * 512],
                                   in_=ost[:, pd, tg, :, :]).then_inc(sOSTp[pd].h, 16)

            @block.sync
            def _(sync):
                for DC in range(KC):
                    b = DC % 2
                    sync.wait_ge(sAG.h, 17)
                    if DC == 0:
                        sync.wait_ge(sUV.h, 2 * HCN)
                    if DC >= 2:
                        sync.wait_ge(sFMM.h, DC - 1)
                    r, dcl = DC // 4, DC % 4
                    sync.dma_start(out=w2v(b), in_=ag_w2_o.ap()[r, dcl]).then_inc(sW2STRp[b].h, 16)
                    if DC >= 7 and (DC - 7) % 4 == 0:
                        emit_out_store(sync, (DC - 7) // 4)
                emit_out_store(sync, 7)
            sW2STRp[0].n += 16 * (KC // 2); sW2STRp[1].n += 16 * (KC // 2)
            sOSTp[0].n += 16 * 4 * 4; sOSTp[1].n += 16 * 4 * 4

            def emit_f_tp(t, DC):
                b = DC % 2
                t.wait_ge(sFEP.h, DC + 1)
                if DC >= 2:
                    t.wait_ge(sOEV.h, DC - 1)
                for tg in range(4):
                    ins = t.transpose(pB.ap()[:, b, tg * 128:(tg + 1) * 128],
                                      tbuf[:, b, tg * 128:(tg + 1) * 128], identf[:])
                ins.then_inc(sOTP.h, 1)

            @block.tensor
            def _(t):
                for DC in range(KC):
                    b = DC % 2
                    t.wait_ge(sW2STRp[b].h, 16 * (DC // 2 + 1))
                    if DC == 0:
                        t.wait_ge(sGATE.h, HCN)
                    if DC >= 2:
                        t.wait_ge(sFEP.h, DC - 1)
                    for k in range(HCN):
                        ins = t.matmul(pA.ap()[:, b, :], w2v(b)[:, k, :], gbuf[:, k, :],
                                       start=(k == 0), stop=(k == HCN - 1))
                    ins.then_inc(sFMM.h, 1)
                    if DC >= 1:
                        emit_f_tp(t, DC - 1)
                emit_f_tp(t, KC - 1)
            sFMM.n += KC
            sOTP.n += KC

            @block.vector
            def _(v):
                for DC in range(KC):
                    b = DC % 2
                    v.wait_ge(sFMM.h, DC + 1)
                    if DC >= 2:
                        v.wait_ge(sOTP.h, DC - 1)
                    v.scalar_tensor_tensor(tbuf[:, b, :], pA.ap()[:, b, :], gb_v[:, DC:DC + 1],
                                           xres[:, DC, :], ALU.mult, ALU.add).then_inc(sFEP.h, 1)
            sFEP.n += KC

            @block.scalar
            def _(s):
                for DC in range(KC):
                    b = DC % 2
                    pd = (DC // 4) % 2
                    if debug and T == 0 and DC == 0:
                        s.wait_ge(sDBG.h, sDBG.n)
                    s.wait_ge(sOTP.h, DC + 1)
                    if DC >= 8:
                        s.wait_ge(sOSTp[(DC // 4) % 2].h, 64 * (DC // 8))
                    s.copy(ost[:, pd, :, DC % 4, :],
                           pB.ap()[:, b, :].rearrange("p (g n) -> p g n", g=4)).then_inc(sOEV.h, 1)
            sOEV.n += KC


            prev = dict(sOSTp=sOSTp, sFMM=sFMM)

        if "sOSTp" in prev:
            @block.gpsimd
            def _(g):
                g.wait_ge(prev["sOSTp"][0].h, prev["sOSTp"][0].n)
                g.wait_ge(prev["sOSTp"][1].h, prev["sOSTp"][1].n)

    return nc


def kernel(L0_attn_data, L0_attn_weight, L0_attn_mask, L0_ffn_w1, L0_ffn_s1,
           L0_ffn_w2, L0_ffn_s2, L0_ffn_w3, L0_ffn_s3, L0_c, L0_wmod):
    x = np.ascontiguousarray(np.asarray(L0_attn_data, np.float32).reshape(8192, D))
    aw = np.asarray(L0_attn_weight, np.float32)
    am = np.asarray(L0_attn_mask, np.int32)
    w1 = np.asarray(L0_ffn_w1, np.int32)
    s1 = np.asarray(L0_ffn_s1, np.float32)
    w2 = np.asarray(L0_ffn_w2, np.int32)
    s2 = np.asarray(L0_ffn_s2, np.float32)
    w3 = np.asarray(L0_ffn_w3, np.int32)
    s3 = np.asarray(L0_ffn_s3, np.float32)
    cv = np.asarray(L0_c, np.float32).reshape(1, D)
    wm = np.asarray(L0_wmod, np.float32)

    nc = build_nc()
    in_maps = []
    for c in range(NCORES):
        im = {}
        im["x"] = np.ascontiguousarray(x[c * 1024:(c + 1) * 1024])
        im["aw"] = np.ascontiguousarray(aw[c * 512:(c + 1) * 512])
        im["am"] = np.ascontiguousarray(am[c * 512:(c + 1) * 512])
        r0 = STARTS[c] * 4096
        r1 = (STARTS[c] + CNTS[c]) * 4096
        pad = (SLOTS - CNTS[c]) * 4096
        im["w1p"] = np.ascontiguousarray(np.pad(w1[r0:r1], ((0, pad), (0, 0))))
        im["w3p"] = np.ascontiguousarray(np.pad(w3[r0:r1], ((0, pad), (0, 0))))
        im["s1p"] = np.ascontiguousarray(np.pad(s1[2 * r0:2 * r1], (0, 2 * pad)))
        im["s3p"] = np.ascontiguousarray(np.pad(s3[2 * r0:2 * r1], (0, 2 * pad)))
        im["w2p"] = np.ascontiguousarray(w2[c * 44032:(c + 1) * 44032])
        im["s2p"] = np.ascontiguousarray(s2[c * 88064:(c + 1) * 88064])
        im["cvec"] = cv
        im["wmod"] = np.ascontiguousarray(wm[:, c * 3072:(c + 1) * 3072])
        in_maps.append(im)

    res = run_bass_kernel_spmd(nc, in_maps, core_ids=list(range(NCORES)))
    full = np.concatenate([np.asarray(r["out"], np.float32) for r in res.results], axis=0)
    return full.reshape(1, 8192, D)


# version 13 (final)
# speedup vs baseline: 1.0160x; 1.0160x over previous
import numpy as np
import ml_dtypes
from contextlib import ExitStack

import concourse.bass as bass
import concourse.mybir as mybir
from concourse.bass_utils import run_bass_kernel_spmd

F32 = mybir.dt.float32
BF16 = mybir.dt.bfloat16
I32 = mybir.dt.int32
ALU = mybir.AluOpType
ACT = mybir.ActivationFunctionType
AX = mybir.AxisListType.X

NCORES = 8
D = 4096
KC = 32
NT = 512
TILES = 2
HCN = 86
SLOTS = 11
STARTS = [0, 11, 22, 33, 44, 55, 66, 76]
CNTS = [11, 11, 11, 11, 11, 11, 10, 10]
HC2RS = {STARTS[r] + s: (r, s) for r in range(8) for s in range(CNTS[r])}
PAIRS = [(0, 1), (2, 3), (4, 5), (6, 7), (8, 9), (10,)]
SLOT_ORDER = []
for pi, pair in enumerate(PAIRS):
    for s in pair:
        for r in range(8):
            if s < CNTS[r]:
                SLOT_ORDER.append((STARTS[r] + s, pi))
assert len(SLOT_ORDER) == HCN


class Sem:
    def __init__(self, nc, ctx, name):
        self.h = ctx.enter_context(nc.semaphore(name))
        self.n = 0

    def inc(self, inst, k=1):
        inst.then_inc(self.h, k)
        self.n += k
        return self.n


def build_nc(phases="full", shared=True, detect_races=True, debug=False):
    nc = bass.Bass("TRN2", target_bir_lowering=False, detect_race_conditions=detect_races)

    x_in = nc.dram_tensor("x", [TILES * NT, D], F32, kind="ExternalInput").ap()
    aw_in = nc.dram_tensor("aw", [512, D], F32, kind="ExternalInput").ap()
    am_in = nc.dram_tensor("am", [512, D], I32, kind="ExternalInput").ap()
    w1p = nc.dram_tensor("w1p", [SLOTS * 4096, 64], I32, kind="ExternalInput").ap()
    s1p = nc.dram_tensor("s1p", [SLOTS * 8192], F32, kind="ExternalInput").ap()
    w3p = nc.dram_tensor("w3p", [SLOTS * 4096, 64], I32, kind="ExternalInput").ap()
    s3p = nc.dram_tensor("s3p", [SLOTS * 8192], F32, kind="ExternalInput").ap()
    w2p = nc.dram_tensor("w2p", [44032, 64], I32, kind="ExternalInput").ap()
    s2p = nc.dram_tensor("s2p", [88064], F32, kind="ExternalInput").ap()
    c_in = nc.dram_tensor("cvec", [1, D], F32, kind="ExternalInput").ap()
    wm_in = nc.dram_tensor("wmod", [D, 3072], F32, kind="ExternalInput").ap()
    out = nc.dram_tensor("out", [TILES * NT, D], F32, kind="ExternalOutput").ap()
    if debug:
        dbg_mod = nc.dram_tensor("dbg_mod", [8, 3072], F32, kind="ExternalOutput").ap()
        dbg_aw = nc.dram_tensor("dbg_aw", [128, KC, 128], BF16, kind="ExternalOutput").ap()
        dbg_w13 = nc.dram_tensor("dbg_w13", [128, KC, 128], BF16, kind="ExternalOutput").ap()
        dbg_w2 = nc.dram_tensor("dbg_w2", [128, HCN, 128], BF16, kind="ExternalOutput").ap()
        dbg_xres = nc.dram_tensor("dbg_xres", [128, KC, NT], BF16, kind="ExternalOutput").ap()
        dbg_h1 = nc.dram_tensor("dbg_h1", [128, KC, NT], BF16, kind="ExternalOutput").ap()
        dbg_x2 = nc.dram_tensor("dbg_x2", [128, KC, NT], BF16, kind="ExternalOutput").ap()
        dbg_h2 = nc.dram_tensor("dbg_h2", [128, KC, NT], BF16, kind="ExternalOutput").ap()
        dbg_g = nc.dram_tensor("dbg_g", [128, 8, NT], BF16, kind="ExternalOutput").ap()

    ag_mod_i = nc.dram_tensor("ag_mod_i", [3072], F32, kind="Internal")
    ag_mod_o = nc.dram_tensor("ag_mod_o", [8, 3072], F32, kind="Internal")
    ag_aw_i = nc.dram_tensor("ag_aw_i", [4, 128, KC, 128], BF16, kind="Internal")
    ag_aw_o = nc.dram_tensor("ag_aw_o", [8, 4, 128, KC, 128], BF16, kind="Internal",
                             addr_space=("Shared" if shared else "Local"))
    ag_w13_i = nc.dram_tensor("ag_w13_i", [SLOTS, 2, 128, KC, 128], BF16, kind="Internal")
    ag_w13_o = []
    for pi, pair in enumerate(PAIRS):
        ag_w13_o.append(nc.dram_tensor(
            f"ag_w13_o{pi}", [8, len(pair), 2, 128, KC, 128], BF16, kind="Internal",
            addr_space=("Shared" if shared else "Local")))
    ag_w2_i = nc.dram_tensor("ag_w2_i", [4, 128, HCN, 128], BF16, kind="Internal")
    ag_w2_o = nc.dram_tensor("ag_w2_o", [8, 4, 128, HCN, 128], BF16, kind="Internal",
                             addr_space=("Shared" if shared else "Local"))
    bar_i = nc.dram_tensor("bar_i", [8], F32, kind="Internal")
    bar_o = nc.dram_tensor("bar_o", [8, 8], F32, kind="Internal")

    identb_c = nc.inline_tensor(np.eye(128, dtype=ml_dtypes.bfloat16), "identb_c")
    identf_c = nc.inline_tensor(np.eye(128, dtype=np.float32), "identf_c")
    ones1_c = nc.inline_tensor(np.ones((1, 128), np.float32), "ones1_c")
    stx_np = np.zeros((128, 2), ml_dtypes.bfloat16); stx_np[:, 0] = 1
    stq_np = np.zeros((128, 2), ml_dtypes.bfloat16); stq_np[:, 1] = 1
    stx_c = nc.inline_tensor(stx_np, "stx_c")
    stq_c = nc.inline_tensor(stq_np, "stq_c")
    bd4_np = np.zeros((128, 4), ml_dtypes.bfloat16)
    for p in range(128):
        bd4_np[p, p // 32] = 1
    bd4_c = nc.inline_tensor(bd4_np, "bd4_c")
    ex4_np = np.zeros((4, 128), ml_dtypes.bfloat16)
    for m in range(128):
        ex4_np[m // 32, m] = 1
    ex4_c = nc.inline_tensor(ex4_np, "ex4_c")

    ctx = ExitStack()
    with ctx:
        sb = lambda *a, **k: ctx.enter_context(nc.sbuf_tensor(*a, **k))
        identb = sb("identb", [128, 128], BF16)
        identf = sb("identf", [128, 128], F32)
        ones1 = sb("ones1", [1, 128], F32)
        stx = sb("stx", [128, 2], BF16)
        stq = sb("stq", [128, 2], BF16)
        bd4 = sb("bd4", [128, 4], BF16)
        ex4 = sb("ex4", [4, 128], BF16)
        ca1 = sb("ca1", [128, KC], F32)
        sa_v = sb("sa_v", [128, KC], F32)
        cb1 = sb("cb1", [128, KC], F32)
        sb_v = sb("sb_v", [128, KC], F32)
        ga_v = sb("ga_v", [128, KC], F32)
        gb_v = sb("gb_v", [128, KC], F32)
        xres = sb("xres", [128, KC, NT], BF16)
        hbuf = sb("hbuf", [128, KC, NT], BF16)
        tbuf = sb("tbuf", [128, 2, NT], F32)
        wsb = sb("wsb", [128, 24576], BF16)

        def awv(b):
            return wsb.ap()[:, b * 4096:(b + 1) * 4096].rearrange("p (K n) -> p K n", K=KC)

        def w1v(b):
            return wsb.ap()[:, 8192 + b * 4096: 8192 + (b + 1) * 4096].rearrange("p (K n) -> p K n", K=KC)

        def w3v(b):
            return wsb.ap()[:, 16384 + b * 4096: 16384 + (b + 1) * 4096].rearrange("p (K n) -> p K n", K=KC)

        def w2v(b):
            return wsb.ap()[:, b * 11008:(b + 1) * 11008].rearrange("p (H d) -> p H d", H=HCN)

        hx = hbuf.ap().rearrange("p c t -> p (c t)").rearrange("p (g k) -> p g k", g=4)
        ost = hbuf.ap().rearrange("p c t -> p (c t)").bitcast(F32)[:, 0:4096].rearrange(
            "p (b g c d) -> p b g c d", b=2, g=4, c=4)

        ps = lambda *a, **k: ctx.enter_context(nc.psum_tensor(*a, **k))
        pS = ps("pS", [128, 512], F32)
        pT = ps("pT", [128, 2, 1024], BF16)

        block = ctx.enter_context(nc.Block())
        sem = lambda name: Sem(nc, ctx, name)

        sCONST = sem("sCONST")
        sXST = sem("sXST")
        sXTP = sem("sXTP")
        sXEV = sem("sXEV")
        sAWL = sem("sAWL")
        sAWC = sem("sAWC")
        sAWM = sem("sAWM")
        sAWE = sem("sAWE")
        sAWSp = [sem("sAWS0"), sem("sAWS1")]
        sPKp = [sem("sPK0"), sem("sPK1")]
        sDQ = sem("sDQ")
        sWTP = sem("sWTP")
        sWEV = sem("sWEV")
        sWSTp = [sem("sWST0"), sem("sWST1")]
        sWMp = [sem(f"sWM{q}") for q in range(4)]
        sMM = sem("sMM")
        sMEV = sem("sMEV")
        sMST = sem("sMST")
        sAG = sem("sAG")
        sMODL = sem("sMODL")
        sMODD = sem("sMODD")

        _sbuf_before_prep = nc.sbuf_bytes_remaining
        with ExitStack() as prep:
            psb = lambda *a, **k: prep.enter_context(nc.sbuf_tensor(*a, **k))
            pk = psb("pk", [128, 2, 2752], I32)
            nibi = psb("nibi", [128, 5504], I32)
            deqb = psb("deqb", [128, 5504], BF16)
            sclf = psb("sclf", [128, 2, 172], F32)
            wmbuf = psb("wmbuf", [128, 4, 512], F32)
            modsb = psb("modsb", [1, 512], F32)
            stage = psb("stage", [128, 2, 1024], BF16)
            maskgf = psb("maskgf", [128, 128], F32)
            maskgb = psb("maskgb", [128, 128], BF16)
            csb = psb("csb", [128, KC], F32)
            silc = psb("silc", [128, KC], F32)
            gt1 = psb("gt1", [128, KC], F32)
            gt2 = psb("gt2", [128, KC], F32)
            bmb = psb("bmb", [4, 128], BF16)
            _arena = _sbuf_before_prep - nc.sbuf_bytes_remaining
            assert _arena <= 77824, f"prep arena {_arena}B overlaps xsq/stt tail"
            papre = prep.enter_context(nc.psum_tensor("papre", [128, 2, 512], F32))
            pmod = prep.enter_context(nc.psum_tensor("pmod", [128, 3, 512], F32))

            amaskv = nibi.ap().bitcast(BF16)[:, 0:4096]
            awrowv = deqb.ap()[:, 0:4096]

            @block.sync
            def _(sync):
                sync.dma_start(out=identb[:], in_=identb_c.ap()).then_inc(sCONST.h, 16); sCONST.n += 16
                sync.dma_start(out=identf[:], in_=identf_c.ap()).then_inc(sCONST.h, 16); sCONST.n += 16
                sync.dma_start(out=ones1[:], in_=ones1_c.ap()).then_inc(sCONST.h, 16); sCONST.n += 16
                sync.dma_start(out=stx[:], in_=stx_c.ap()).then_inc(sCONST.h, 16); sCONST.n += 16
                sync.dma_start(out=stq[:], in_=stq_c.ap()).then_inc(sCONST.h, 16); sCONST.n += 16
                sync.dma_start(out=bd4[:], in_=bd4_c.ap()).then_inc(sCONST.h, 16); sCONST.n += 16
                sync.dma_start(out=ex4[:], in_=ex4_c.ap()).then_inc(sCONST.h, 16); sCONST.n += 16
                sync.dma_start(out=csb[:], in_=c_in.rearrange("o (ch p) -> (o p) ch", p=128), allow_slow_non_contiguous=True).then_inc(sCONST.h, 16); sCONST.n += 16

            @block.gpsimd
            def _(g):
                g.dma_start(
                    out=hx, in_=x_in.rearrange("(T g p) k -> T g p k", T=TILES, g=4, p=128)[0].transpose([1, 0, 2])
                ).then_inc(sXST.h, 16); sXST.n += 16
                for rt in range(4):
                    if rt > 0:
                        g.wait_ge(sAWC.h, rt * 4)
                    g.dma_start(out=awrowv, in_=aw_in[rt * 128:(rt + 1) * 128, :]).then_inc(sAWL.h, 16); sAWL.n += 16
                    g.dma_start(out=amaskv, in_=am_in[rt * 128:(rt + 1) * 128, :]).then_inc(sAWL.h, 16); sAWL.n += 16
                    if rt < 3:
                        g.wait_ge(sAWM.h, (rt + 1) * 2 + (rt + 1) * 4)
                wmt = wm_in.rearrange("(kt p) (cc n) -> cc kt p n", p=128, n=512)
                for i in range(6 * KC):
                    cc, kt = divmod(i, KC)
                    if i >= 4:
                        g.wait_ge(sMM.h, i - 3)
                    g.dma_start(out=wmbuf[:, i % 4, :], in_=wmt[cc, kt]).then_inc(sWMp[i % 4].h, 16); sWMp[i % 4].n += 16
                    if kt == 3 and cc >= 1:
                        g.wait_ge(sMEV.h, cc + 1)
                        g.dma_start(out=ag_mod_i.ap()[(cc - 1) * 512:cc * 512],
                                    in_=modsb.ap()).then_inc(sMST.h, 16); sMST.n += 16
                g.wait_ge(sMEV.h, 7)
                g.dma_start(out=ag_mod_i.ap()[5 * 512:6 * 512],
                            in_=modsb.ap()).then_inc(sMST.h, 16); sMST.n += 16

            @block.scalar
            def _(s):
                s.wait_ge(sCONST.h, sCONST.n)
                s.activation(silc[:], csb[:], ACT.Silu).then_inc(sMEV.h, 1); sMEV.n += 1

            @block.tensor
            def _(t):
                t.wait_ge(sCONST.h, sCONST.n)
                t.wait_ge(sXST.h, 16)
                for kc in range(KC):
                    par = kc % 2
                    if kc >= 2:
                        t.wait_ge(sXEV.h, kc - 1)
                    for tg in range(4):
                        ins = t.transpose(pT.ap()[:, par, tg * 128:(tg + 1) * 128],
                                          hx[:, tg, kc * 128:(kc + 1) * 128], identb[:])
                    ins.then_inc(sXTP.h, 1); sXTP.n += 1

            @block.scalar
            def _(s):
                for kc in range(KC):
                    par = kc % 2
                    s.wait_ge(sXTP.h, kc + 1)
                    s.copy(xres[:, kc, :], pT.ap()[:, par, 0:512]).then_inc(sXEV.h, 1); sXEV.n += 1


            @block.vector
            def _(v):
                for rt in range(4):
                    v.wait_ge(sAWL.h, 32 * (rt + 1))
                    if rt >= 1:
                        v.wait_ge(sAWM.h, (rt - 1) * 6 + 1)
                    v.reduce_sum(maskgf[:], amaskv.rearrange("p (g w) -> p g w", w=32), axis=AX).then_inc(sAWC.h, 1); sAWC.n += 1
                    v.tensor_copy(maskgb[:], maskgf[:]).then_inc(sAWC.h, 1); sAWC.n += 1
                    v.wait_ge(sAWM.h, rt * 6 + 1)
                    v.tensor_scalar(bmb[:], papre.ap()[0:4, rt % 2, 0:128], 0.0, None, ALU.is_gt).then_inc(sAWC.h, 1); sAWC.n += 1
                    v.wait_ge(sAWM.h, rt * 6 + 2)
                    v.tensor_tensor(
                        awrowv.rearrange("p (kb w) -> p kb w", w=32),
                        awrowv.rearrange("p (kb w) -> p kb w", w=32),
                        papre.ap()[:, rt % 2, 128:256].rearrange("p (kb o) -> p kb o", o=1).broadcast_to([128, 128, 32]),
                        ALU.mult,
                    ).then_inc(sAWC.h, 1); sAWC.n += 1

            @block.tensor
            def _(t):
                for rt in range(4):
                    b = rt % 2
                    t.wait_ge(sAWC.h, rt * 4 + 2)
                    if rt >= 2:
                        t.wait_ge(sAWC.h, (rt - 1) * 4 + 4)
                    t.matmul(papre.ap()[0:4, b, 0:128], bd4[:], maskgb[:], start=True, stop=True).then_inc(sAWM.h, 1); sAWM.n += 1
                    t.wait_ge(sAWC.h, rt * 4 + 3)
                    t.matmul(papre.ap()[:, b, 128:256], ex4[:], bmb[:], start=True, stop=True).then_inc(sAWM.h, 1); sAWM.n += 1
                    t.wait_ge(sAWC.h, rt * 4 + 4)
                    if rt == 0:
                        t.wait_ge(sXEV.h, KC)
                    for pk8 in range(4):
                        tp = (rt * 4 + pk8) % 2
                        if rt * 4 + pk8 >= 2:
                            t.wait_ge(sAWE.h, rt * 4 + pk8 - 1)
                        for j in range(8):
                            kt = pk8 * 8 + j
                            ins = t.transpose(pT.ap()[:, tp, j * 128:(j + 1) * 128],
                                              awrowv[:, kt * 128:(kt + 1) * 128], identb[:])
                        ins.then_inc(sAWM.h, 1); sAWM.n += 1

            @block.scalar
            def _(s):
                for i in range(16):
                    rt, pk8 = divmod(i, 4)
                    tp = i % 2
                    s.wait_ge(sAWM.h, rt * 6 + 2 + pk8 + 1)
                    if i >= 2:
                        s.wait_ge(sAWSp[i % 2].h, 16 * (i // 2))
                    s.copy(stage[:, tp, :], pT.ap()[:, tp, :]).then_inc(sAWE.h, 1); sAWE.n += 1

            @block.sync
            def _(sync):
                for i in range(16):
                    rt, pk8 = divmod(i, 4)
                    tp = i % 2
                    sync.wait_ge(sAWE.h, i + 1)
                    sync.dma_start(
                        out=ag_aw_i.ap()[rt, :, pk8 * 8:(pk8 + 1) * 8, :],
                        in_=stage[:, tp, :].rearrange("p (K n) -> p K n", K=8),
                    ).then_inc(sAWSp[i % 2].h, 16); sAWSp[i % 2].n += 16

            dq_tiles = []
            for s in range(SLOTS):
                dq_tiles.append(("w13", 0, s))
                dq_tiles.append(("w13", 1, s))
            for dt in range(4):
                for hf in range(2):
                    dq_tiles.append(("w2", dt, hf))

            w1pr = w1p.rearrange("(s p j) c -> s p (j c)", s=SLOTS, p=128, j=32)
            w3pr = w3p.rearrange("(s p j) c -> s p (j c)", s=SLOTS, p=128, j=32)
            s1pr = s1p.rearrange("(s p g) -> s p g", s=SLOTS, p=128)
            s3pr = s3p.rearrange("(s p g) -> s p g", s=SLOTS, p=128)
            w2pr = w2p.rearrange("(dt p j) c -> dt p (j c)", dt=4, p=128, j=86)
            s2pr = s2p.rearrange("(dt p g) -> dt p g", dt=4, p=128)

            def npacks(kind):
                return 4 if kind == "w13" else 6

            total_packs = sum(npacks(t[0]) for t in dq_tiles)
            pack_at_tile = []
            _pc = 0
            for _kind, _a, _b in dq_tiles:
                _pc += npacks(_kind)
                pack_at_tile.append(_pc)

            def emit_dq_store(sync, i):
                kind, a, bidx = dq_tiles[i]
                base = pack_at_tile[i - 1] if i >= 1 else 0
                nkt = KC if kind == "w13" else 43
                for p8 in range(npacks(kind)):
                    pkc = base + p8
                    tp = pkc % 2
                    sync.wait_ge(sWEV.h, pkc + 1)
                    nk8 = min((p8 + 1) * 8, nkt) - p8 * 8
                    if kind == "w13":
                        o = ag_w13_i.ap()[bidx, a, :, p8 * 8:p8 * 8 + nk8, :]
                    else:
                        o = ag_w2_i.ap()[a, :, bidx * 43 + p8 * 8: bidx * 43 + p8 * 8 + nk8, :]
                    sync.dma_start(out=o, in_=stage[:, tp, 0:nk8 * 128].rearrange("p (K n) -> p K n", K=nk8)).then_inc(sWSTp[pkc % 2].h, 16); sWSTp[pkc % 2].n += 16

            @block.sync
            def _(sync):
                for i, (kind, a, bidx) in enumerate(dq_tiles):
                    par = i % 2
                    if i >= 2:
                        sync.wait_ge(sDQ.h, i - 1)
                    if kind == "w13":
                        src = (w1pr if a == 0 else w3pr)[bidx]
                        sync.dma_start(out=pk[:, par, 0:2048], in_=src).then_inc(sPKp[par].h, 16); sPKp[par].n += 16
                        ssrc = (s1pr if a == 0 else s3pr)[bidx]
                        sync.dma_start(out=sclf[:, par, 0:64], in_=ssrc).then_inc(sPKp[par].h, 16); sPKp[par].n += 16
                    else:
                        sync.dma_start(out=pk[:, par, :], in_=w2pr[a][:, bidx * 2752:(bidx + 1) * 2752]).then_inc(sPKp[par].h, 16); sPKp[par].n += 16
                        sync.dma_start(out=sclf[:, par, 0:86], in_=s2pr[a][:, bidx * 86:(bidx + 1) * 86]).then_inc(sPKp[par].h, 16); sPKp[par].n += 16
                    if i >= 1:
                        emit_dq_store(sync, i - 1)
                emit_dq_store(sync, len(dq_tiles) - 1)

            @block.vector
            def _(v):
                for i, (kind, a, bidx) in enumerate(dq_tiles):
                    par = i % 2
                    if i == 0:
                        v.wait_ge(sAWM.h, 24)
                    if i >= 4 and (i - 4) % 3 == 0 and (i - 4) // 3 < 6:
                        cc = (i - 4) // 3
                        v.wait_ge(sMM.h, KC * (cc + 1))
                        if cc >= 1:
                            v.wait_ge(sMST.h, 16 * cc)
                        v.tensor_copy(modsb.ap(), pmod.ap()[0:1, cc % 3, :]).then_inc(sMEV.h, 1); sMEV.n += 1
                    v.wait_ge(sPKp[par].h, 32 * (i // 2 + 1))
                    if i >= 1:
                        v.wait_ge(sWTP.h, pack_at_tile[i - 1])
                    if kind == "w13":
                        nv = nibi.ap()[:, 0:4096].rearrange("p (j two c) -> p j two c", two=2, c=64)
                        pkv = pk.ap()[:, par, 0:2048].rearrange("p (j c) -> p j c", c=64)
                        v.tensor_scalar(nv[:, :, 0, :], pkv, 4, None, ALU.arith_shift_right)
                        v.tensor_scalar(nv[:, :, 1, :], pkv, 28, 28, ALU.arith_shift_left, ALU.arith_shift_right)
                        v.tensor_tensor(
                            deqb.ap()[:, 0:4096].rearrange("p (g c) -> p g c", c=64),
                            nibi.ap()[:, 0:4096].rearrange("p (g c) -> p g c", c=64),
                            sclf.ap()[:, par, 0:64].unsqueeze(2).broadcast_to([128, 64, 64]),
                            ALU.mult,
                        ).then_inc(sDQ.h, 1); sDQ.n += 1
                    else:
                        nv = nibi.ap().rearrange("p (j two c) -> p j two c", two=2, c=64)
                        pkv = pk.ap()[:, par, :].rearrange("p (j c) -> p j c", c=64)
                        v.tensor_scalar(nv[:, :, 0, :], pkv, 4, None, ALU.arith_shift_right)
                        v.tensor_scalar(nv[:, :, 1, :], pkv, 28, 28, ALU.arith_shift_left, ALU.arith_shift_right)
                        v.tensor_tensor(
                            deqb.ap().rearrange("p (g c) -> p g c", c=64),
                            nibi.ap().rearrange("p (g c) -> p g c", c=64),
                            sclf.ap()[:, par, 0:86].unsqueeze(2).broadcast_to([128, 86, 64]),
                            ALU.mult,
                        ).then_inc(sDQ.h, 1); sDQ.n += 1

            def emit_mod_group(t, cc):
                if cc == 0:
                    t.wait_ge(sMEV.h, 1)
                for kt in range(KC):
                    i = cc * KC + kt
                    t.wait_ge(sWMp[i % 4].h, 16 * (i // 4 + 1))
                    if kt == 0 and cc >= 3:
                        t.wait_ge(sMST.h, 16 * (cc - 2))
                    t.matmul(pmod.ap()[0:1, cc % 3, :], silc[:, kt:kt + 1], wmbuf[:, i % 4, :],
                             start=(kt == 0), stop=(kt == KC - 1)).then_inc(sMM.h, 1); sMM.n += 1

            MOD_AFTER = {2: 0, 5: 1, 8: 2, 11: 3, 14: 4, 17: 5}

            @block.tensor
            def _(t):
                pkc = 0
                for i, (kind, a, bidx) in enumerate(dq_tiles):
                    t.wait_ge(sDQ.h, i + 1)
                    nkt = KC if kind == "w13" else 43
                    for p8 in range(npacks(kind)):
                        tp = pkc % 2
                        if pkc < 2:
                            t.wait_ge(sAWE.h, 16)
                        else:
                            t.wait_ge(sWEV.h, pkc - 1)
                        js = range(p8 * 8, min((p8 + 1) * 8, nkt))
                        for jj, kt in enumerate(js):
                            ins = t.transpose(pT.ap()[:, tp, jj * 128:(jj + 1) * 128],
                                              deqb.ap()[:, kt * 128:(kt + 1) * 128], identb[:])
                        ins.then_inc(sWTP.h, 1); sWTP.n += 1
                        pkc += 1
                    if i in MOD_AFTER:
                        emit_mod_group(t, MOD_AFTER[i])

            @block.scalar
            def _(s):
                pkc = 0
                for i, (kind, a, bidx) in enumerate(dq_tiles):
                    for p8 in range(npacks(kind)):
                        tp = pkc % 2
                        s.wait_ge(sWTP.h, pkc + 1)
                        if pkc >= 2:
                            s.wait_ge(sWSTp[pkc % 2].h, 16 * (pkc // 2))
                        nkt = KC if kind == "w13" else 43
                        w = (min((p8 + 1) * 8, nkt) - p8 * 8) * 128
                        s.copy(stage[:, tp, 0:w], pT.ap()[:, tp, 0:w]).then_inc(sWEV.h, 1); sWEV.n += 1
                        pkc += 1


            @block.gpsimd
            def _(g):
                rg = [list(range(NCORES))]

                def barrier():
                    g.wait_ge(sAG.h, sAG.n)
                    g.collective_compute("AllGather", ALU.bypass, replica_groups=rg,
                                         ins=[bar_i.ap().opt()], outs=[bar_o.ap().opt()]).then_inc(sAG.h, 1); sAG.n += 1

                g.wait_ge(sMST.h, 96)
                g.collective_compute("AllGather", ALU.bypass, replica_groups=rg,
                                     ins=[ag_mod_i.ap().opt()], outs=[ag_mod_o.ap().opt()]).then_inc(sAG.h, 1); sAG.n += 1
                g.wait_ge(sAG.h, 1)
                flatm = ag_mod_o.ap().rearrange("r x -> (r x)").rearrange(
                    "(v ch p) -> v p ch", v=6, p=128)
                for vi, dst in enumerate([sa_v, ca1, gt1, sb_v, cb1, gt2]):
                    g.dma_start(out=dst[:], in_=flatm[vi], allow_slow_non_contiguous=True).then_inc(sMODL.h, 16); sMODL.n += 16
                g.wait_ge(sAWSp[0].h, sAWSp[0].n)
                g.wait_ge(sAWSp[1].h, sAWSp[1].n)
                g.wait_ge(sAG.h, 1)
                g.collective_compute("AllGather", ALU.bypass, replica_groups=rg,
                                     ins=[ag_aw_i.ap().opt()], outs=[ag_aw_o.ap().opt()]).then_inc(sAG.h, 1); sAG.n += 1
                barrier()
                for pi, pair in enumerate(PAIRS):
                    last_slot = pair[-1]
                    tile_idx = 2 * last_slot + 1
                    np_tot = pack_at_tile[tile_idx]
                    g.wait_ge(sWSTp[0].h, 16 * ((np_tot + 1) // 2))
                    g.wait_ge(sWSTp[1].h, 16 * (np_tot // 2))
                    g.wait_ge(sAG.h, sAG.n)
                    g.collective_compute("AllGather", ALU.bypass, replica_groups=rg,
                                         ins=[ag_w13_i.ap()[pair[0]:last_slot + 1].opt()],
                                         outs=[ag_w13_o[pi].ap().opt()]).then_inc(sAG.h, 1); sAG.n += 1
                    barrier()
                g.wait_ge(sWSTp[0].h, 16 * ((total_packs + 1) // 2))
                g.wait_ge(sWSTp[1].h, 16 * (total_packs // 2))
                g.wait_ge(sAG.h, sAG.n)
                g.collective_compute("AllGather", ALU.bypass, replica_groups=rg,
                                     ins=[ag_w2_i.ap().opt()], outs=[ag_w2_o.ap().opt()]).then_inc(sAG.h, 1); sAG.n += 1
                barrier()
                if debug:
                    g.wait_ge(sAG.h, 17)
                    g.dma_start(out=dbg_mod, in_=ag_mod_o.ap()).then_inc(sAG.h, 16)
                    g.dma_start(out=dbg_aw, in_=ag_aw_o.ap()[0, 0]).then_inc(sAG.h, 16)
                    g.dma_start(out=dbg_w13, in_=ag_w13_o[0].ap()[0, 0, 0]).then_inc(sAG.h, 16)
                    g.dma_start(out=dbg_w2, in_=ag_w2_o.ap()[0, 0]).then_inc(sAG.h, 16)
                    g.wait_ge(sAG.h, 17 + 64)
                    sAG.n += 64

            @block.vector
            def _(v):
                v.wait_ge(sMODL.h, 16 * 6)
                v.tensor_scalar(ca1[:], ca1[:], 1.0, None, ALU.add)
                v.tensor_scalar(cb1[:], cb1[:], 1.0, None, ALU.add)
                v.tensor_scalar(ga_v[:], gt1[:], 0.0, None, ALU.is_ge)
                v.tensor_scalar(gb_v[:], gt2[:], 0.0, None, ALU.is_ge)
                v.tensor_scalar(ga_v[:], ga_v[:], 2.0, None, ALU.mult)
                v.tensor_scalar(gb_v[:], gb_v[:], 2.0, None, ALU.mult)
                v.tensor_scalar(ga_v[:], ga_v[:], -1.0, None, ALU.add)
                v.tensor_scalar(gb_v[:], gb_v[:], -1.0, None, ALU.add)
                nc.vector.engine_nop().then_inc(sMODD.h, 1); sMODD.n += 1

            @block.scalar
            def _(s):
                s.wait_ge(sMODD.h, 1)
                s.activation(gt1[:], gt1[:], ACT.Abs)
                s.activation(gt2[:], gt2[:], ACT.Abs).then_inc(sMODD.h, 1); sMODD.n += 1

            @block.vector
            def _(v):
                v.wait_ge(sMODD.h, 2)
                v.tensor_scalar(gt1[:], gt1[:], 0.25, None, ALU.max)
                v.tensor_scalar(gt2[:], gt2[:], 0.25, None, ALU.max)
                v.tensor_tensor(ga_v[:], ga_v[:], gt1[:], ALU.mult)
                v.tensor_tensor(gb_v[:], gb_v[:], gt2[:], ALU.mult).then_inc(sMODD.h, 1); sMODD.n += 1

        if phases == "prep":
            @block.gpsimd
            def _(g):
                g.wait_ge(sAG.h, sAG.n)
            @block.sync
            def _(sync):
                sync.wait_ge(sMODL.h, sMODL.n)
            return nc

        gbuf = sb("gbuf", [128, HCN, NT], BF16)
        gflat = gbuf.ap().rearrange("p h t -> p (h t)")
        stt = gflat.bitcast(F32)[0:1, 19968:22016]
        xsq = gflat[:, 38912:39936].rearrange("p (a t) -> p a t", a=2)
        pA = ps("pA", [128, 2, 512], F32)
        pB = ps("pB", [128, 2, 512], F32)

        prev = {}
        ntiles = 1 if phases == "t0" else TILES
        for T in range(ntiles):
            u = f"t{T}_"
            sSQ = sem(u + "sSQ")
            sSTAT = sem(u + "sSTAT")
            sSS = sem(u + "sSS")
            sBC = sem(u + "sBC")
            sLN = sem(u + "sLN")
            sLNT = sem(u + "sLNT")
            sASTRp = [sem(u + "sASTR0"), sem(u + "sASTR1")]
            sAMM = sem(u + "sAMM")
            sAEP = sem(u + "sAEP")
            sSQ2 = sem(u + "sSQ2")
            sSTAT2 = sem(u + "sSTAT2")
            sSS2 = sem(u + "sSS2")
            sBC2 = sem(u + "sBC2")
            sLN2 = sem(u + "sLN2")
            sLNT2 = sem(u + "sLNT2")
            sWSTRp = [sem(u + "sWSTR0"), sem(u + "sWSTR1")]
            sUV = sem(u + "sUV")
            sSIL = sem(u + "sSIL")
            sGATE = sem(u + "sGATE")
            sW2STRp = [sem(u + "sW2STR0"), sem(u + "sW2STR1")]
            sFMM = sem(u + "sFMM")
            sFEP = sem(u + "sFEP")
            sOTP = sem(u + "sOTP")
            sOEV = sem(u + "sOEV")
            sOSTp = [sem(u + "sOST0"), sem(u + "sOST1")]
            sXST2 = sem(u + "sXST2")
            sXTP2 = sem(u + "sXTP2")
            sXEV2 = sem(u + "sXEV2")

            if T == 1:
                @block.gpsimd
                def _(g, T=T):
                    g.wait_ge(prev["sOSTp"][0].h, prev["sOSTp"][0].n)
                    g.wait_ge(prev["sOSTp"][1].h, prev["sOSTp"][1].n)
                    g.dma_start(
                        out=hx, in_=x_in.rearrange("(T g p) k -> T g p k", T=TILES, g=4, p=128)[T].transpose([1, 0, 2])
                    ).then_inc(sXST2.h, 16)
                sXST2.n += 16

                @block.tensor
                def _(t, T=T):
                    t.wait_ge(sXST2.h, 16)
                    for kc in range(KC):
                        par = kc % 2
                        if kc >= 2:
                            t.wait_ge(sXEV2.h, kc - 1)
                        for tg in range(4):
                            ins = t.transpose(pT.ap()[:, par, tg * 128:(tg + 1) * 128],
                                              hx[:, tg, kc * 128:(kc + 1) * 128], identb[:])
                        ins.then_inc(sXTP2.h, 1)
                sXTP2.n += KC

                @block.scalar
                def _(s, T=T):
                    for kc in range(KC):
                        par = kc % 2
                        s.wait_ge(sXTP2.h, kc + 1)
                        s.copy(xres[:, kc, :], pT.ap()[:, par, 0:512]).then_inc(sXEV2.h, 1)
                sXEV2.n += KC
                sXEVcur, sXEVn = sXEV2, KC
            else:
                sXEVcur, sXEVn = sXEV, KC

            def emit_ln(sSQ_, sSTAT_, sSS_, sBC_, sLN_, sLNT_, scale_v, bias_v,
                        wait_chunk=None, xev=None, xevn=0):
                @block.scalar
                def _(s):
                    for c in range(KC):
                        par = c % 2
                        if xev is not None:
                            s.wait_ge(xev.h, min(c + 1, xevn))
                        if wait_chunk is not None:
                            s.wait_ge(wait_chunk.h, wait_chunk.n - KC + c + 1)
                        if c >= 2:
                            s.wait_ge(sSTAT_.h, c - 1)
                        s.activation(xsq[:, par, :], xres[:, c, :], ACT.Square).then_inc(sSQ_.h, 1)
                sSQ_.n += KC

                @block.tensor
                def _(t):
                    t.wait_ge(sCONST.h, sCONST.n)
                    for c in range(KC):
                        par = c % 2
                        if xev is not None:
                            t.wait_ge(xev.h, min(c + 1, xevn))
                        if wait_chunk is not None:
                            t.wait_ge(wait_chunk.h, wait_chunk.n - KC + c + 1)
                        t.matmul(pS.ap()[0:1, :], stx[:, 0:1], xres[:, c, :], start=(c == 0), stop=(c == KC - 1))
                        t.wait_ge(sSQ_.h, c + 1)
                        ins = t.matmul(pB.ap()[0:1, 0, :], stx[:, 0:1], xsq[:, par, :], start=(c == 0), stop=(c == KC - 1))
                        ins.then_inc(sSTAT_.h, 1)
                sSTAT_.n += KC

                if phases == "stats":
                    @block.gpsimd
                    def _(g):
                        g.wait_ge(sSTAT_.h, KC)
                    return

                @block.vector
                def _(v):
                    v.wait_ge(sSTAT_.h, KC)
                    rstd = stt[:, 0:NT]; nmr = stt[:, NT:2 * NT]
                    mu = stt[:, 2 * NT:3 * NT]; tmp = stt[:, 3 * NT:4 * NT]
                    v.tensor_scalar(mu, pS.ap()[0:1, :], 1.0 / D, None, ALU.mult)
                    v.tensor_scalar(tmp, pB.ap()[0:1, 0, :], 1.0 / D, None, ALU.mult)
                    v.tensor_tensor(rstd, mu, mu, ALU.mult)
                    v.tensor_tensor(tmp, tmp, rstd, ALU.subtract)
                    nc.vector.engine_nop().then_inc(sSS_.h, 1)
                sSS_.n += 1

                @block.scalar
                def _(s):
                    s.wait_ge(sSS_.h, 1)
                    s.activation(stt[:, 3 * NT:4 * NT], stt[:, 3 * NT:4 * NT], ACT.Sqrt, bias=np.float32(1e-5)).then_inc(sSS_.h, 1)
                sSS_.n += 1

                @block.vector
                def _(v):
                    v.wait_ge(sSS_.h, 2)
                    rstd = stt[:, 0:NT]; nmr = stt[:, NT:2 * NT]
                    mu = stt[:, 2 * NT:3 * NT]; tmp = stt[:, 3 * NT:4 * NT]
                    v.reciprocal(rstd, tmp)
                    v.scalar_tensor_tensor(nmr, mu, -1.0, rstd, ALU.mult, ALU.mult)
                    nc.vector.engine_nop().then_inc(sSS_.h, 1)
                sSS_.n += 1

                @block.tensor
                def _(t):
                    t.wait_ge(sSS_.h, 3)
                    t.matmul(pB.ap()[:, 0, :], ones1[:], stt[:, 0:NT], start=True, stop=True)
                    t.matmul(pB.ap()[:, 1, :], ones1[:], stt[:, NT:2 * NT], start=True, stop=True).then_inc(sBC_.h, 1)
                sBC_.n += 1

                if phases == "bcast":
                    @block.gpsimd
                    def _(g):
                        g.wait_ge(sBC_.h, 1)
                    return

                @block.vector
                def _(v):
                    v.wait_ge(sBC_.h, 1)
                    v.wait_ge(sMODD.h, 3)
                    for c in range(KC):
                        par = c % 2
                        if c >= 2:
                            v.wait_ge(sLN_.h, c - 1)
                        v.tensor_tensor(tbuf[:, par, :], xres[:, c, :], pB.ap()[:, 0, :], ALU.mult)
                        v.tensor_tensor(tbuf[:, par, :], tbuf[:, par, :], pB.ap()[:, 1, :], ALU.add).then_inc(sLNT_.h, 1)
                sLNT_.n += KC

                @block.scalar
                def _(s):
                    for c in range(KC):
                        par = c % 2
                        s.wait_ge(sLNT_.h, c + 1)
                        s.activation(hbuf[:, c, :], tbuf[:, par, :], ACT.Identity,
                                     scale=scale_v[:, c:c + 1], bias=bias_v[:, c:c + 1]).then_inc(sLN_.h, 1)
                sLN_.n += KC

            if debug and T == 0:
                sDBG = sem(u + "sDBG")

                @block.sync
                def _(sync):
                    sync.wait_ge(sXEV.h, KC)
                    sync.dma_start(out=dbg_xres, in_=xres[:]).then_inc(sDBG.h, 16)
                sDBG.n += 16

            emit_ln(sSQ, sSTAT, sSS, sBC, sLN, sLNT, ca1, sa_v,
                    wait_chunk=None, xev=sXEVcur, xevn=sXEVn)

            if debug and T == 0:
                @block.sync
                def _(sync):
                    sync.wait_ge(sLN.h, KC)
                    sync.dma_start(out=dbg_h1, in_=hbuf[:]).then_inc(sDBG.h, 16)
                sDBG.n += 16

            if phases in ("stats", "bcast"):
                break

            if phases == "ln1":
                @block.gpsimd
                def _(g):
                    g.wait_ge(sLN.h, KC)
                break

            @block.sync
            def _(sync):
                for NCc in range(KC):
                    b = NCc % 2
                    sync.wait_ge(sAG.h, 3)
                    if NCc >= 2:
                        sync.wait_ge(sAMM.h, NCc - 1)
                    if T == 1 and NCc < 2:
                        sync.wait_ge(prev["sFMM"].h, prev["sFMM"].n)
                    r, ncl = NCc // 4, NCc % 4
                    sync.dma_start(out=awv(b), in_=ag_aw_o.ap()[r, ncl]).then_inc(sASTRp[b].h, 16)
            sASTRp[0].n += 16 * (KC // 2); sASTRp[1].n += 16 * (KC // 2)

            @block.tensor
            def _(t):
                for NCc in range(KC):
                    b = NCc % 2
                    t.wait_ge(sASTRp[b].h, 16 * (NCc // 2 + 1))
                    t.wait_ge(sLN.h, KC)
                    if NCc >= 2:
                        t.wait_ge(sAEP.h, NCc - 1)
                    for k in range(KC):
                        ins = t.matmul(pA.ap()[:, b, :], awv(b)[:, k, :], hbuf[:, k, :],
                                       start=(k == 0), stop=(k == KC - 1))
                    ins.then_inc(sAMM.h, 1)
            sAMM.n += KC

            @block.vector
            def _(v):
                for NCc in range(KC):
                    b = NCc % 2
                    v.wait_ge(sAMM.h, NCc + 1)
                    if debug and T == 0 and NCc == 0:
                        v.wait_ge(sDBG.h, 32)
                    v.scalar_tensor_tensor(xres[:, NCc, :], pA.ap()[:, b, :], ga_v[:, NCc:NCc + 1],
                                           xres[:, NCc, :], ALU.mult, ALU.add).then_inc(sAEP.h, 1)
            sAEP.n += KC

            if phases == "attn":
                @block.gpsimd
                def _(g):
                    g.wait_ge(sAEP.h, KC)
                break

            if debug and T == 0:
                @block.sync
                def _(sync):
                    sync.wait_ge(sAEP.h, KC)
                    sync.dma_start(out=dbg_x2, in_=xres[:]).then_inc(sDBG.h, 16)
                sDBG.n += 16

            emit_ln(sSQ2, sSTAT2, sSS2, sBC2, sLN2, sLNT2, cb1, sb_v,
                    wait_chunk=sAEP, xev=None, xevn=0)

            if debug and T == 0:
                @block.sync
                def _(sync):
                    sync.wait_ge(sLN2.h, KC)
                    sync.dma_start(out=dbg_h2, in_=hbuf[:]).then_inc(sDBG.h, 16)
                sDBG.n += 16

            if phases == "ln2":
                @block.gpsimd
                def _(g):
                    g.wait_ge(sLN2.h, KC)
                break

            @block.sync
            def _(sync):
                for i, (hc, pi) in enumerate(SLOT_ORDER):
                    b = i % 2
                    sync.wait_ge(sAG.h, 5 + 2 * pi)
                    if i >= 2:
                        sync.wait_ge(sUV.h, 2 * (i - 2) + 2)
                    r, s = HC2RS[hc]
                    pair = PAIRS[pi]
                    sub = s - pair[0]
                    sync.dma_start(out=w1v(b), in_=ag_w13_o[pi].ap()[r, sub, 0]).then_inc(sWSTRp[b].h, 16)
                    sync.dma_start(out=w3v(b), in_=ag_w13_o[pi].ap()[r, sub, 1]).then_inc(sWSTRp[b].h, 16)
            sWSTRp[0].n += 32 * (HCN // 2 + HCN % 2); sWSTRp[1].n += 32 * (HCN // 2)

            @block.tensor
            def _(t):
                for i, (hc, pi) in enumerate(SLOT_ORDER):
                    b = i % 2
                    t.wait_ge(sWSTRp[b].h, 32 * (i // 2 + 1))
                    t.wait_ge(sLN2.h, KC)
                    if i >= 2:
                        t.wait_ge(sSIL.h, i - 1)
                        t.wait_ge(sGATE.h, i - 1)
                    for k in range(KC):
                        ins = t.matmul(pA.ap()[:, b, :], w1v(b)[:, k, :], hbuf[:, k, :],
                                       start=(k == 0), stop=(k == KC - 1))
                    ins.then_inc(sUV.h, 1)
                    for k in range(KC):
                        ins = t.matmul(pB.ap()[:, b, :], w3v(b)[:, k, :], hbuf[:, k, :],
                                       start=(k == 0), stop=(k == KC - 1))
                    ins.then_inc(sUV.h, 1)
            sUV.n += 2 * HCN

            @block.scalar
            def _(s):
                for i, (hc, pi) in enumerate(SLOT_ORDER):
                    b = i % 2
                    s.wait_ge(sUV.h, 2 * i + 1)
                    if i >= 2:
                        s.wait_ge(sGATE.h, i - 1)
                    s.activation(tbuf[:, b, :], pA.ap()[:, b, :], ACT.Silu).then_inc(sSIL.h, 1)
            sSIL.n += HCN

            @block.vector
            def _(v):
                for i, (hc, pi) in enumerate(SLOT_ORDER):
                    b = i % 2
                    v.wait_ge(sUV.h, 2 * i + 2)
                    v.wait_ge(sSIL.h, i + 1)
                    v.tensor_tensor(gbuf[:, hc, :], tbuf[:, b, :], pB.ap()[:, b, :], ALU.mult).then_inc(sGATE.h, 1)
            sGATE.n += HCN

            if phases == "uv":
                @block.gpsimd
                def _(g):
                    g.wait_ge(sGATE.h, HCN)
                break

            if debug and T == 0:
                @block.sync
                def _(sync):
                    sync.wait_ge(sGATE.h, HCN)
                    sync.dma_start(out=dbg_g, in_=gbuf[:, 0:8, :]).then_inc(sDBG.h, 16)
                sDBG.n += 16

            outr = out.rearrange("(Tt g p) d -> Tt g p d", Tt=TILES, g=4, p=128)

            def emit_out_store(sync, DS):
                pd = DS % 2
                sync.wait_ge(sOEV.h, 4 * (DS + 1))
                for tg in range(4):
                    sync.dma_start(out=outr[T, tg][:, DS * 512:(DS + 1) * 512],
                                   in_=ost[:, pd, tg, :, :]).then_inc(sOSTp[pd].h, 16)

            @block.sync
            def _(sync):
                for DC in range(KC):
                    b = DC % 2
                    sync.wait_ge(sAG.h, 17)
                    if DC == 0:
                        sync.wait_ge(sUV.h, 2 * HCN)
                    if DC >= 2:
                        sync.wait_ge(sFMM.h, DC - 1)
                    r, dcl = DC // 4, DC % 4
                    sync.dma_start(out=w2v(b), in_=ag_w2_o.ap()[r, dcl]).then_inc(sW2STRp[b].h, 16)
                    if DC >= 7 and (DC - 7) % 4 == 0:
                        emit_out_store(sync, (DC - 7) // 4)
                emit_out_store(sync, 7)
            sW2STRp[0].n += 16 * (KC // 2); sW2STRp[1].n += 16 * (KC // 2)
            sOSTp[0].n += 16 * 4 * 4; sOSTp[1].n += 16 * 4 * 4

            def emit_f_tp(t, DC):
                b = DC % 2
                t.wait_ge(sFEP.h, DC + 1)
                if DC >= 2:
                    t.wait_ge(sOEV.h, DC - 1)
                for tg in range(4):
                    ins = t.transpose(pB.ap()[:, b, tg * 128:(tg + 1) * 128],
                                      tbuf[:, b, tg * 128:(tg + 1) * 128], identf[:])
                ins.then_inc(sOTP.h, 1)

            @block.tensor
            def _(t):
                for DC in range(KC):
                    b = DC % 2
                    t.wait_ge(sW2STRp[b].h, 16 * (DC // 2 + 1))
                    if DC == 0:
                        t.wait_ge(sGATE.h, HCN)
                    if DC >= 2:
                        t.wait_ge(sFEP.h, DC - 1)
                    for k in range(HCN):
                        ins = t.matmul(pA.ap()[:, b, :], w2v(b)[:, k, :], gbuf[:, k, :],
                                       start=(k == 0), stop=(k == HCN - 1))
                    ins.then_inc(sFMM.h, 1)
                    if DC >= 1:
                        emit_f_tp(t, DC - 1)
                emit_f_tp(t, KC - 1)
            sFMM.n += KC
            sOTP.n += KC

            @block.vector
            def _(v):
                for DC in range(KC):
                    b = DC % 2
                    v.wait_ge(sFMM.h, DC + 1)
                    if DC >= 2:
                        v.wait_ge(sOTP.h, DC - 1)
                    v.scalar_tensor_tensor(tbuf[:, b, :], pA.ap()[:, b, :], gb_v[:, DC:DC + 1],
                                           xres[:, DC, :], ALU.mult, ALU.add).then_inc(sFEP.h, 1)
            sFEP.n += KC

            @block.scalar
            def _(s):
                for DC in range(KC):
                    b = DC % 2
                    pd = (DC // 4) % 2
                    if debug and T == 0 and DC == 0:
                        s.wait_ge(sDBG.h, sDBG.n)
                    s.wait_ge(sOTP.h, DC + 1)
                    if DC >= 8:
                        s.wait_ge(sOSTp[(DC // 4) % 2].h, 64 * (DC // 8))
                    s.copy(ost[:, pd, :, DC % 4, :],
                           pB.ap()[:, b, :].rearrange("p (g n) -> p g n", g=4)).then_inc(sOEV.h, 1)
            sOEV.n += KC


            prev = dict(sOSTp=sOSTp, sFMM=sFMM)

        if "sOSTp" in prev:
            @block.gpsimd
            def _(g):
                g.wait_ge(prev["sOSTp"][0].h, prev["sOSTp"][0].n)
                g.wait_ge(prev["sOSTp"][1].h, prev["sOSTp"][1].n)

    return nc


def kernel(L0_attn_data, L0_attn_weight, L0_attn_mask, L0_ffn_w1, L0_ffn_s1,
           L0_ffn_w2, L0_ffn_s2, L0_ffn_w3, L0_ffn_s3, L0_c, L0_wmod):
    x = np.ascontiguousarray(np.asarray(L0_attn_data, np.float32).reshape(8192, D))
    aw = np.asarray(L0_attn_weight, np.float32)
    am = np.asarray(L0_attn_mask, np.int32)
    w1 = np.asarray(L0_ffn_w1, np.int32)
    s1 = np.asarray(L0_ffn_s1, np.float32)
    w2 = np.asarray(L0_ffn_w2, np.int32)
    s2 = np.asarray(L0_ffn_s2, np.float32)
    w3 = np.asarray(L0_ffn_w3, np.int32)
    s3 = np.asarray(L0_ffn_s3, np.float32)
    cv = np.asarray(L0_c, np.float32).reshape(1, D)
    wm = np.asarray(L0_wmod, np.float32)

    nc = build_nc()
    in_maps = []
    for c in range(NCORES):
        im = {}
        im["x"] = np.ascontiguousarray(x[c * 1024:(c + 1) * 1024])
        im["aw"] = np.ascontiguousarray(aw[c * 512:(c + 1) * 512])
        im["am"] = np.ascontiguousarray(am[c * 512:(c + 1) * 512])
        r0 = STARTS[c] * 4096
        r1 = (STARTS[c] + CNTS[c]) * 4096
        pad = (SLOTS - CNTS[c]) * 4096
        im["w1p"] = np.ascontiguousarray(np.pad(w1[r0:r1], ((0, pad), (0, 0))))
        im["w3p"] = np.ascontiguousarray(np.pad(w3[r0:r1], ((0, pad), (0, 0))))
        im["s1p"] = np.ascontiguousarray(np.pad(s1[2 * r0:2 * r1], (0, 2 * pad)))
        im["s3p"] = np.ascontiguousarray(np.pad(s3[2 * r0:2 * r1], (0, 2 * pad)))
        im["w2p"] = np.ascontiguousarray(w2[c * 44032:(c + 1) * 44032])
        im["s2p"] = np.ascontiguousarray(s2[c * 88064:(c + 1) * 88064])
        im["cvec"] = cv
        im["wmod"] = np.ascontiguousarray(wm[:, c * 3072:(c + 1) * 3072])
        in_maps.append(im)

    res = run_bass_kernel_spmd(nc, in_maps, core_ids=list(range(NCORES)))
    full = np.concatenate([np.asarray(r["out"], np.float32) for r in res.results], axis=0)
    return full.reshape(1, 8192, D)


# version 15
# speedup vs baseline: 1.0162x; 1.0002x over previous
import numpy as np
import ml_dtypes
from contextlib import ExitStack

import concourse.bass as bass
import concourse.mybir as mybir
from concourse.bass_utils import run_bass_kernel_spmd

F32 = mybir.dt.float32
BF16 = mybir.dt.bfloat16
I32 = mybir.dt.int32
ALU = mybir.AluOpType
ACT = mybir.ActivationFunctionType
AX = mybir.AxisListType.X

NCORES = 8
D = 4096
KC = 32
NT = 512
TILES = 2
HCN = 86
SLOTS = 11
STARTS = [0, 11, 22, 33, 44, 55, 66, 76]
CNTS = [11, 11, 11, 11, 11, 11, 10, 10]
HC2RS = {STARTS[r] + s: (r, s) for r in range(8) for s in range(CNTS[r])}
PAIRS = [(0, 1), (2, 3), (4, 5), (6, 7), (8, 9), (10,)]
SLOT_ORDER = []
for pi, pair in enumerate(PAIRS):
    for s in pair:
        for r in range(8):
            if s < CNTS[r]:
                SLOT_ORDER.append((STARTS[r] + s, pi))
assert len(SLOT_ORDER) == HCN


class Sem:
    def __init__(self, nc, ctx, name):
        self.h = ctx.enter_context(nc.semaphore(name))
        self.n = 0

    def inc(self, inst, k=1):
        inst.then_inc(self.h, k)
        self.n += k
        return self.n


def build_nc(phases="full", shared=True, detect_races=True, debug=False):
    nc = bass.Bass("TRN2", target_bir_lowering=False, detect_race_conditions=detect_races)

    x_in = nc.dram_tensor("x", [TILES * NT, D], F32, kind="ExternalInput").ap()
    aw_in = nc.dram_tensor("aw", [512, D], F32, kind="ExternalInput").ap()
    am_in = nc.dram_tensor("am", [512, D], I32, kind="ExternalInput").ap()
    w1p = nc.dram_tensor("w1p", [SLOTS * 4096, 64], I32, kind="ExternalInput").ap()
    s1p = nc.dram_tensor("s1p", [SLOTS * 8192], F32, kind="ExternalInput").ap()
    w3p = nc.dram_tensor("w3p", [SLOTS * 4096, 64], I32, kind="ExternalInput").ap()
    s3p = nc.dram_tensor("s3p", [SLOTS * 8192], F32, kind="ExternalInput").ap()
    w2p = nc.dram_tensor("w2p", [44032, 64], I32, kind="ExternalInput").ap()
    s2p = nc.dram_tensor("s2p", [88064], F32, kind="ExternalInput").ap()
    c_in = nc.dram_tensor("cvec", [1, D], F32, kind="ExternalInput").ap()
    wm_in = nc.dram_tensor("wmod", [D, 3072], F32, kind="ExternalInput").ap()
    out = nc.dram_tensor("out", [TILES * NT, D], F32, kind="ExternalOutput").ap()
    if debug:
        dbg_mod = nc.dram_tensor("dbg_mod", [8, 3072], F32, kind="ExternalOutput").ap()
        dbg_aw = nc.dram_tensor("dbg_aw", [128, KC, 128], BF16, kind="ExternalOutput").ap()
        dbg_w13 = nc.dram_tensor("dbg_w13", [128, KC, 128], BF16, kind="ExternalOutput").ap()
        dbg_w2 = nc.dram_tensor("dbg_w2", [128, HCN, 128], BF16, kind="ExternalOutput").ap()
        dbg_xres = nc.dram_tensor("dbg_xres", [128, KC, NT], BF16, kind="ExternalOutput").ap()
        dbg_h1 = nc.dram_tensor("dbg_h1", [128, KC, NT], BF16, kind="ExternalOutput").ap()
        dbg_x2 = nc.dram_tensor("dbg_x2", [128, KC, NT], BF16, kind="ExternalOutput").ap()
        dbg_h2 = nc.dram_tensor("dbg_h2", [128, KC, NT], BF16, kind="ExternalOutput").ap()
        dbg_g = nc.dram_tensor("dbg_g", [128, 8, NT], BF16, kind="ExternalOutput").ap()

    ag_mod_i = nc.dram_tensor("ag_mod_i", [3072], F32, kind="Internal")
    ag_mod_o = nc.dram_tensor("ag_mod_o", [8, 3072], F32, kind="Internal")
    ag_aw_i = nc.dram_tensor("ag_aw_i", [4, 128, KC, 128], BF16, kind="Internal")
    ag_aw_o = nc.dram_tensor("ag_aw_o", [8, 4, 128, KC, 128], BF16, kind="Internal",
                             addr_space=("Shared" if shared else "Local"))
    ag_w13_i = nc.dram_tensor("ag_w13_i", [SLOTS, 2, 128, KC, 128], BF16, kind="Internal")
    ag_w13_o = []
    for pi, pair in enumerate(PAIRS):
        ag_w13_o.append(nc.dram_tensor(
            f"ag_w13_o{pi}", [8, len(pair), 2, 128, KC, 128], BF16, kind="Internal",
            addr_space=("Shared" if shared else "Local")))
    ag_w2_i = nc.dram_tensor("ag_w2_i", [4, 128, HCN, 128], BF16, kind="Internal")
    ag_w2_o = nc.dram_tensor("ag_w2_o", [8, 4, 128, HCN, 128], BF16, kind="Internal",
                             addr_space=("Shared" if shared else "Local"))
    bar_i = nc.dram_tensor("bar_i", [8], F32, kind="Internal")
    bar_o = nc.dram_tensor("bar_o", [8, 8], F32, kind="Internal")

    identb_c = nc.inline_tensor(np.eye(128, dtype=ml_dtypes.bfloat16), "identb_c")
    identf_c = nc.inline_tensor(np.eye(128, dtype=np.float32), "identf_c")
    ones1_c = nc.inline_tensor(np.ones((1, 128), np.float32), "ones1_c")
    stx_np = np.zeros((128, 2), ml_dtypes.bfloat16); stx_np[:, 0] = 1
    stq_np = np.zeros((128, 2), ml_dtypes.bfloat16); stq_np[:, 1] = 1
    stx_c = nc.inline_tensor(stx_np, "stx_c")
    stq_c = nc.inline_tensor(stq_np, "stq_c")
    bd4_np = np.zeros((128, 4), ml_dtypes.bfloat16)
    for p in range(128):
        bd4_np[p, p // 32] = 1
    bd4_c = nc.inline_tensor(bd4_np, "bd4_c")
    ex4_np = np.zeros((4, 128), ml_dtypes.bfloat16)
    for m in range(128):
        ex4_np[m // 32, m] = 1
    ex4_c = nc.inline_tensor(ex4_np, "ex4_c")

    ctx = ExitStack()
    with ctx:
        sb = lambda *a, **k: ctx.enter_context(nc.sbuf_tensor(*a, **k))
        identb = sb("identb", [128, 128], BF16)
        identf = sb("identf", [128, 128], F32)
        ones1 = sb("ones1", [1, 128], F32)
        stx = sb("stx", [128, 2], BF16)
        stq = sb("stq", [128, 2], BF16)
        bd4 = sb("bd4", [128, 4], BF16)
        ex4 = sb("ex4", [4, 128], BF16)
        ca1 = sb("ca1", [128, KC], F32)
        sa_v = sb("sa_v", [128, KC], F32)
        cb1 = sb("cb1", [128, KC], F32)
        sb_v = sb("sb_v", [128, KC], F32)
        ga_v = sb("ga_v", [128, KC], F32)
        gb_v = sb("gb_v", [128, KC], F32)
        xres = sb("xres", [128, KC, NT], BF16)
        hbuf = sb("hbuf", [128, KC, NT], BF16)
        tbuf = sb("tbuf", [128, 2, NT], F32)
        wsb = sb("wsb", [128, 24576], BF16)

        def awv(b):
            return wsb.ap()[:, b * 4096:(b + 1) * 4096].rearrange("p (K n) -> p K n", K=KC)

        def w1v(b):
            return wsb.ap()[:, 8192 + b * 4096: 8192 + (b + 1) * 4096].rearrange("p (K n) -> p K n", K=KC)

        def w3v(b):
            return wsb.ap()[:, 16384 + b * 4096: 16384 + (b + 1) * 4096].rearrange("p (K n) -> p K n", K=KC)

        def w2v(b):
            return wsb.ap()[:, b * 11008:(b + 1) * 11008].rearrange("p (H d) -> p H d", H=HCN)

        hx = hbuf.ap().rearrange("p c t -> p (c t)").rearrange("p (g k) -> p g k", g=4)
        ost = hbuf.ap().rearrange("p c t -> p (c t)").bitcast(F32)[:, 0:4096].rearrange(
            "p (b g c d) -> p b g c d", b=2, g=4, c=4)

        ps = lambda *a, **k: ctx.enter_context(nc.psum_tensor(*a, **k))
        pS = ps("pS", [128, 512], F32)
        pT = ps("pT", [128, 2, 1024], BF16)

        block = ctx.enter_context(nc.Block())
        sem = lambda name: Sem(nc, ctx, name)

        sCONST = sem("sCONST")
        sXST = sem("sXST")
        sXTP = sem("sXTP")
        sXEV = sem("sXEV")
        sAWL = sem("sAWL")
        sAWC = sem("sAWC")
        sAWM = sem("sAWM")
        sAWE = sem("sAWE")
        sAWSp = [sem("sAWS0"), sem("sAWS1")]
        sPKp = [sem("sPK0"), sem("sPK1")]
        sDQ = sem("sDQ")
        sWTP = sem("sWTP")
        sWEV = sem("sWEV")
        sWSTp = [sem("sWST0"), sem("sWST1")]
        sWMp = [sem(f"sWM{q}") for q in range(4)]
        sMM = sem("sMM")
        sMEV = sem("sMEV")
        sMST = sem("sMST")
        sAG = sem("sAG")
        sMODL = sem("sMODL")
        sMODD = sem("sMODD")

        _sbuf_before_prep = nc.sbuf_bytes_remaining
        with ExitStack() as prep:
            psb = lambda *a, **k: prep.enter_context(nc.sbuf_tensor(*a, **k))
            pk = psb("pk", [128, 2, 2752], I32)
            nibi = psb("nibi", [128, 5504], I32)
            deqb = psb("deqb", [128, 5504], BF16)
            sclf = psb("sclf", [128, 2, 172], F32)
            wmbuf = psb("wmbuf", [128, 4, 512], F32)
            modsb = psb("modsb", [1, 512], F32)
            stage = psb("stage", [128, 2, 1024], BF16)
            maskgf = psb("maskgf", [128, 128], F32)
            maskgb = psb("maskgb", [128, 128], BF16)
            csb = psb("csb", [128, KC], F32)
            silc = psb("silc", [128, KC], F32)
            gt1 = psb("gt1", [128, KC], F32)
            gt2 = psb("gt2", [128, KC], F32)
            bmb = psb("bmb", [4, 128], BF16)
            _arena = _sbuf_before_prep - nc.sbuf_bytes_remaining
            assert _arena <= 77824, f"prep arena {_arena}B overlaps xsq/stt tail"
            papre = prep.enter_context(nc.psum_tensor("papre", [128, 2, 512], F32))
            pmod = prep.enter_context(nc.psum_tensor("pmod", [128, 3, 512], F32))

            amaskv = nibi.ap().bitcast(BF16)[:, 0:4096]
            awrowv = deqb.ap()[:, 0:4096]

            @block.sync
            def _(sync):
                sync.dma_start(out=identb[:], in_=identb_c.ap()).then_inc(sCONST.h, 16); sCONST.n += 16
                sync.dma_start(out=identf[:], in_=identf_c.ap()).then_inc(sCONST.h, 16); sCONST.n += 16
                sync.dma_start(out=ones1[:], in_=ones1_c.ap()).then_inc(sCONST.h, 16); sCONST.n += 16
                sync.dma_start(out=stx[:], in_=stx_c.ap()).then_inc(sCONST.h, 16); sCONST.n += 16
                sync.dma_start(out=stq[:], in_=stq_c.ap()).then_inc(sCONST.h, 16); sCONST.n += 16
                sync.dma_start(out=bd4[:], in_=bd4_c.ap()).then_inc(sCONST.h, 16); sCONST.n += 16
                sync.dma_start(out=ex4[:], in_=ex4_c.ap()).then_inc(sCONST.h, 16); sCONST.n += 16
                sync.dma_start(out=csb[:], in_=c_in.rearrange("o (ch p) -> (o p) ch", p=128), allow_slow_non_contiguous=True).then_inc(sCONST.h, 16); sCONST.n += 16

            @block.gpsimd
            def _(g):
                g.dma_start(
                    out=hx, in_=x_in.rearrange("(T g p) k -> T g p k", T=TILES, g=4, p=128)[0].transpose([1, 0, 2])
                ).then_inc(sXST.h, 16); sXST.n += 16
                for rt in range(4):
                    if rt > 0:
                        g.wait_ge(sAWC.h, rt * 4)
                    g.dma_start(out=awrowv, in_=aw_in[rt * 128:(rt + 1) * 128, :]).then_inc(sAWL.h, 16); sAWL.n += 16
                    g.dma_start(out=amaskv, in_=am_in[rt * 128:(rt + 1) * 128, :]).then_inc(sAWL.h, 16); sAWL.n += 16
                    if rt < 3:
                        g.wait_ge(sAWM.h, (rt + 1) * 2 + (rt + 1) * 4)
                wmt = wm_in.rearrange("(kt p) (cc n) -> cc kt p n", p=128, n=512)
                for i in range(6 * KC):
                    cc, kt = divmod(i, KC)
                    if i >= 4:
                        g.wait_ge(sMM.h, i - 3)
                    g.dma_start(out=wmbuf[:, i % 4, :], in_=wmt[cc, kt]).then_inc(sWMp[i % 4].h, 16); sWMp[i % 4].n += 16
                    if kt == 3 and cc >= 1:
                        g.wait_ge(sMEV.h, cc + 1)
                        g.dma_start(out=ag_mod_i.ap()[(cc - 1) * 512:cc * 512],
                                    in_=modsb.ap()).then_inc(sMST.h, 16); sMST.n += 16
                g.wait_ge(sMEV.h, 7)
                g.dma_start(out=ag_mod_i.ap()[5 * 512:6 * 512],
                            in_=modsb.ap()).then_inc(sMST.h, 16); sMST.n += 16

            @block.scalar
            def _(s):
                s.wait_ge(sCONST.h, sCONST.n)
                s.activation(silc[:], csb[:], ACT.Silu).then_inc(sMEV.h, 1); sMEV.n += 1

            @block.tensor
            def _(t):
                t.wait_ge(sCONST.h, sCONST.n)
                t.wait_ge(sXST.h, 16)
                for kc in range(KC):
                    par = kc % 2
                    if kc >= 2:
                        t.wait_ge(sXEV.h, kc - 1)
                    for tg in range(4):
                        ins = t.transpose(pT.ap()[:, par, tg * 128:(tg + 1) * 128],
                                          hx[:, tg, kc * 128:(kc + 1) * 128], identb[:])
                    ins.then_inc(sXTP.h, 1); sXTP.n += 1

            @block.scalar
            def _(s):
                for kc in range(KC):
                    par = kc % 2
                    s.wait_ge(sXTP.h, kc + 1)
                    s.copy(xres[:, kc, :], pT.ap()[:, par, 0:512]).then_inc(sXEV.h, 1); sXEV.n += 1


            @block.vector
            def _(v):
                for rt in range(4):
                    v.wait_ge(sAWL.h, 32 * (rt + 1))
                    if rt >= 1:
                        v.wait_ge(sAWM.h, (rt - 1) * 6 + 1)
                    v.reduce_sum(maskgf[:], amaskv.rearrange("p (g w) -> p g w", w=32), axis=AX).then_inc(sAWC.h, 1); sAWC.n += 1
                    v.tensor_copy(maskgb[:], maskgf[:]).then_inc(sAWC.h, 1); sAWC.n += 1
                    v.wait_ge(sAWM.h, rt * 6 + 1)
                    v.tensor_scalar(bmb[:], papre.ap()[0:4, rt % 2, 0:128], 0.0, None, ALU.is_gt).then_inc(sAWC.h, 1); sAWC.n += 1
                    v.wait_ge(sAWM.h, rt * 6 + 2)
                    v.tensor_tensor(
                        awrowv.rearrange("p (kb w) -> p kb w", w=32),
                        awrowv.rearrange("p (kb w) -> p kb w", w=32),
                        papre.ap()[:, rt % 2, 128:256].rearrange("p (kb o) -> p kb o", o=1).broadcast_to([128, 128, 32]),
                        ALU.mult,
                    ).then_inc(sAWC.h, 1); sAWC.n += 1

            @block.tensor
            def _(t):
                for rt in range(4):
                    b = rt % 2
                    t.wait_ge(sAWC.h, rt * 4 + 2)
                    if rt >= 2:
                        t.wait_ge(sAWC.h, (rt - 1) * 4 + 4)
                    t.matmul(papre.ap()[0:4, b, 0:128], bd4[:], maskgb[:], start=True, stop=True).then_inc(sAWM.h, 1); sAWM.n += 1
                    t.wait_ge(sAWC.h, rt * 4 + 3)
                    t.matmul(papre.ap()[:, b, 128:256], ex4[:], bmb[:], start=True, stop=True).then_inc(sAWM.h, 1); sAWM.n += 1
                    t.wait_ge(sAWC.h, rt * 4 + 4)
                    if rt == 0:
                        t.wait_ge(sXEV.h, KC)
                    for pk8 in range(4):
                        tp = (rt * 4 + pk8) % 2
                        if rt * 4 + pk8 >= 2:
                            t.wait_ge(sAWE.h, rt * 4 + pk8 - 1)
                        for j in range(8):
                            kt = pk8 * 8 + j
                            ins = t.transpose(pT.ap()[:, tp, j * 128:(j + 1) * 128],
                                              awrowv[:, kt * 128:(kt + 1) * 128], identb[:])
                        ins.then_inc(sAWM.h, 1); sAWM.n += 1

            @block.scalar
            def _(s):
                for i in range(16):
                    rt, pk8 = divmod(i, 4)
                    tp = i % 2
                    s.wait_ge(sAWM.h, rt * 6 + 2 + pk8 + 1)
                    if i >= 2:
                        s.wait_ge(sAWSp[i % 2].h, 16 * (i // 2))
                    s.copy(stage[:, tp, :], pT.ap()[:, tp, :]).then_inc(sAWE.h, 1); sAWE.n += 1

            @block.sync
            def _(sync):
                for i in range(16):
                    rt, pk8 = divmod(i, 4)
                    tp = i % 2
                    sync.wait_ge(sAWE.h, i + 1)
                    sync.dma_start(
                        out=ag_aw_i.ap()[rt, :, pk8 * 8:(pk8 + 1) * 8, :],
                        in_=stage[:, tp, :].rearrange("p (K n) -> p K n", K=8),
                    ).then_inc(sAWSp[i % 2].h, 16); sAWSp[i % 2].n += 16

            dq_tiles = []
            for s in range(SLOTS):
                dq_tiles.append(("w13", 0, s))
                dq_tiles.append(("w13", 1, s))
            for dt in range(4):
                for hf in range(2):
                    dq_tiles.append(("w2", dt, hf))

            w1pr = w1p.rearrange("(s p j) c -> s p (j c)", s=SLOTS, p=128, j=32)
            w3pr = w3p.rearrange("(s p j) c -> s p (j c)", s=SLOTS, p=128, j=32)
            s1pr = s1p.rearrange("(s p g) -> s p g", s=SLOTS, p=128)
            s3pr = s3p.rearrange("(s p g) -> s p g", s=SLOTS, p=128)
            w2pr = w2p.rearrange("(dt p j) c -> dt p (j c)", dt=4, p=128, j=86)
            s2pr = s2p.rearrange("(dt p g) -> dt p g", dt=4, p=128)

            def npacks(kind):
                return 4 if kind == "w13" else 6

            total_packs = sum(npacks(t[0]) for t in dq_tiles)
            pack_at_tile = []
            _pc = 0
            for _kind, _a, _b in dq_tiles:
                _pc += npacks(_kind)
                pack_at_tile.append(_pc)

            def emit_dq_store(sync, i):
                kind, a, bidx = dq_tiles[i]
                base = pack_at_tile[i - 1] if i >= 1 else 0
                nkt = KC if kind == "w13" else 43
                for p8 in range(npacks(kind)):
                    pkc = base + p8
                    tp = pkc % 2
                    sync.wait_ge(sWEV.h, pkc + 1)
                    nk8 = min((p8 + 1) * 8, nkt) - p8 * 8
                    if kind == "w13":
                        o = ag_w13_i.ap()[bidx, a, :, p8 * 8:p8 * 8 + nk8, :]
                    else:
                        o = ag_w2_i.ap()[a, :, bidx * 43 + p8 * 8: bidx * 43 + p8 * 8 + nk8, :]
                    sync.dma_start(out=o, in_=stage[:, tp, 0:nk8 * 128].rearrange("p (K n) -> p K n", K=nk8)).then_inc(sWSTp[pkc % 2].h, 16); sWSTp[pkc % 2].n += 16

            @block.sync
            def _(sync):
                for i, (kind, a, bidx) in enumerate(dq_tiles):
                    par = i % 2
                    if i >= 2:
                        sync.wait_ge(sDQ.h, i - 1)
                    if kind == "w13":
                        src = (w1pr if a == 0 else w3pr)[bidx]
                        sync.dma_start(out=pk[:, par, 0:2048], in_=src).then_inc(sPKp[par].h, 16); sPKp[par].n += 16
                        ssrc = (s1pr if a == 0 else s3pr)[bidx]
                        sync.dma_start(out=sclf[:, par, 0:64], in_=ssrc).then_inc(sPKp[par].h, 16); sPKp[par].n += 16
                    else:
                        sync.dma_start(out=pk[:, par, :], in_=w2pr[a][:, bidx * 2752:(bidx + 1) * 2752]).then_inc(sPKp[par].h, 16); sPKp[par].n += 16
                        sync.dma_start(out=sclf[:, par, 0:86], in_=s2pr[a][:, bidx * 86:(bidx + 1) * 86]).then_inc(sPKp[par].h, 16); sPKp[par].n += 16
                    if i >= 1:
                        emit_dq_store(sync, i - 1)
                emit_dq_store(sync, len(dq_tiles) - 1)

            @block.vector
            def _(v):
                for i, (kind, a, bidx) in enumerate(dq_tiles):
                    par = i % 2
                    if i == 0:
                        v.wait_ge(sAWM.h, 24)
                    if i >= 4 and (i - 4) % 3 == 0 and (i - 4) // 3 < 6:
                        cc = (i - 4) // 3
                        v.wait_ge(sMM.h, KC * (cc + 1))
                        if cc >= 1:
                            v.wait_ge(sMST.h, 16 * cc)
                        v.tensor_copy(modsb.ap(), pmod.ap()[0:1, cc % 3, :]).then_inc(sMEV.h, 1); sMEV.n += 1
                    v.wait_ge(sPKp[par].h, 32 * (i // 2 + 1))
                    if i >= 1:
                        v.wait_ge(sWTP.h, pack_at_tile[i - 1])
                    if kind == "w13":
                        nv = nibi.ap()[:, 0:4096].rearrange("p (j two c) -> p j two c", two=2, c=64)
                        pkv = pk.ap()[:, par, 0:2048].rearrange("p (j c) -> p j c", c=64)
                        v.tensor_scalar(nv[:, :, 0, :], pkv, 4, None, ALU.arith_shift_right)
                        v.tensor_scalar(nv[:, :, 1, :], pkv, 28, 28, ALU.arith_shift_left, ALU.arith_shift_right)
                        v.tensor_tensor(
                            deqb.ap()[:, 0:4096].rearrange("p (g c) -> p g c", c=64),
                            nibi.ap()[:, 0:4096].rearrange("p (g c) -> p g c", c=64),
                            sclf.ap()[:, par, 0:64].unsqueeze(2).broadcast_to([128, 64, 64]),
                            ALU.mult,
                        ).then_inc(sDQ.h, 1); sDQ.n += 1
                    else:
                        nv = nibi.ap().rearrange("p (j two c) -> p j two c", two=2, c=64)
                        pkv = pk.ap()[:, par, :].rearrange("p (j c) -> p j c", c=64)
                        v.tensor_scalar(nv[:, :, 0, :], pkv, 4, None, ALU.arith_shift_right)
                        v.tensor_scalar(nv[:, :, 1, :], pkv, 28, 28, ALU.arith_shift_left, ALU.arith_shift_right)
                        v.tensor_tensor(
                            deqb.ap().rearrange("p (g c) -> p g c", c=64),
                            nibi.ap().rearrange("p (g c) -> p g c", c=64),
                            sclf.ap()[:, par, 0:86].unsqueeze(2).broadcast_to([128, 86, 64]),
                            ALU.mult,
                        ).then_inc(sDQ.h, 1); sDQ.n += 1

            def emit_mod_group(t, cc):
                if cc == 0:
                    t.wait_ge(sMEV.h, 1)
                for kt in range(KC):
                    i = cc * KC + kt
                    t.wait_ge(sWMp[i % 4].h, 16 * (i // 4 + 1))
                    if kt == 0 and cc >= 3:
                        t.wait_ge(sMST.h, 16 * (cc - 2))
                    t.matmul(pmod.ap()[0:1, cc % 3, :], silc[:, kt:kt + 1], wmbuf[:, i % 4, :],
                             start=(kt == 0), stop=(kt == KC - 1)).then_inc(sMM.h, 1); sMM.n += 1

            MOD_AFTER = {2: 0, 5: 1, 8: 2, 11: 3, 14: 4, 17: 5}

            @block.tensor
            def _(t):
                pkc = 0
                for i, (kind, a, bidx) in enumerate(dq_tiles):
                    t.wait_ge(sDQ.h, i + 1)
                    nkt = KC if kind == "w13" else 43
                    for p8 in range(npacks(kind)):
                        tp = pkc % 2
                        if pkc < 2:
                            t.wait_ge(sAWE.h, 16)
                        else:
                            t.wait_ge(sWEV.h, pkc - 1)
                        js = range(p8 * 8, min((p8 + 1) * 8, nkt))
                        for jj, kt in enumerate(js):
                            ins = t.transpose(pT.ap()[:, tp, jj * 128:(jj + 1) * 128],
                                              deqb.ap()[:, kt * 128:(kt + 1) * 128], identb[:])
                        ins.then_inc(sWTP.h, 1); sWTP.n += 1
                        pkc += 1
                    if i in MOD_AFTER:
                        emit_mod_group(t, MOD_AFTER[i])

            @block.scalar
            def _(s):
                pkc = 0
                for i, (kind, a, bidx) in enumerate(dq_tiles):
                    for p8 in range(npacks(kind)):
                        tp = pkc % 2
                        s.wait_ge(sWTP.h, pkc + 1)
                        if pkc >= 2:
                            s.wait_ge(sWSTp[pkc % 2].h, 16 * (pkc // 2))
                        nkt = KC if kind == "w13" else 43
                        w = (min((p8 + 1) * 8, nkt) - p8 * 8) * 128
                        s.copy(stage[:, tp, 0:w], pT.ap()[:, tp, 0:w]).then_inc(sWEV.h, 1); sWEV.n += 1
                        pkc += 1


            @block.gpsimd
            def _(g):
                rg = [list(range(NCORES))]

                def barrier():
                    g.wait_ge(sAG.h, sAG.n)
                    g.collective_compute("AllGather", ALU.bypass, replica_groups=rg,
                                         ins=[bar_i.ap().opt()], outs=[bar_o.ap().opt()]).then_inc(sAG.h, 1); sAG.n += 1

                g.wait_ge(sMST.h, 96)
                g.collective_compute("AllGather", ALU.bypass, replica_groups=rg,
                                     ins=[ag_mod_i.ap().opt()], outs=[ag_mod_o.ap().opt()]).then_inc(sAG.h, 1); sAG.n += 1
                g.wait_ge(sAG.h, 1)
                flatm = ag_mod_o.ap().rearrange("r x -> (r x)").rearrange(
                    "(v ch p) -> v p ch", v=6, p=128)
                for vi, dst in enumerate([sa_v, ca1, gt1, sb_v, cb1, gt2]):
                    g.dma_start(out=dst[:], in_=flatm[vi], allow_slow_non_contiguous=True).then_inc(sMODL.h, 16); sMODL.n += 16
                g.wait_ge(sAWSp[0].h, sAWSp[0].n)
                g.wait_ge(sAWSp[1].h, sAWSp[1].n)
                g.wait_ge(sAG.h, 1)
                g.collective_compute("AllGather", ALU.bypass, replica_groups=rg,
                                     ins=[ag_aw_i.ap().opt()], outs=[ag_aw_o.ap().opt()]).then_inc(sAG.h, 1); sAG.n += 1
                barrier()
                for pi, pair in enumerate(PAIRS):
                    last_slot = pair[-1]
                    tile_idx = 2 * last_slot + 1
                    np_tot = pack_at_tile[tile_idx]
                    g.wait_ge(sWSTp[0].h, 16 * ((np_tot + 1) // 2))
                    g.wait_ge(sWSTp[1].h, 16 * (np_tot // 2))
                    g.wait_ge(sAG.h, sAG.n)
                    g.collective_compute("AllGather", ALU.bypass, replica_groups=rg,
                                         ins=[ag_w13_i.ap()[pair[0]:last_slot + 1].opt()],
                                         outs=[ag_w13_o[pi].ap().opt()]).then_inc(sAG.h, 1); sAG.n += 1
                barrier()
                g.wait_ge(sWSTp[0].h, 16 * ((total_packs + 1) // 2))
                g.wait_ge(sWSTp[1].h, 16 * (total_packs // 2))
                g.wait_ge(sAG.h, sAG.n)
                g.collective_compute("AllGather", ALU.bypass, replica_groups=rg,
                                     ins=[ag_w2_i.ap().opt()], outs=[ag_w2_o.ap().opt()]).then_inc(sAG.h, 1); sAG.n += 1
                barrier()
                if debug:
                    g.wait_ge(sAG.h, 12)
                    g.dma_start(out=dbg_mod, in_=ag_mod_o.ap()).then_inc(sAG.h, 16)
                    g.dma_start(out=dbg_aw, in_=ag_aw_o.ap()[0, 0]).then_inc(sAG.h, 16)
                    g.dma_start(out=dbg_w13, in_=ag_w13_o[0].ap()[0, 0, 0]).then_inc(sAG.h, 16)
                    g.dma_start(out=dbg_w2, in_=ag_w2_o.ap()[0, 0]).then_inc(sAG.h, 16)
                    g.wait_ge(sAG.h, 12 + 64)
                    sAG.n += 64

            @block.vector
            def _(v):
                v.wait_ge(sMODL.h, 16 * 6)
                v.tensor_scalar(ca1[:], ca1[:], 1.0, None, ALU.add)
                v.tensor_scalar(cb1[:], cb1[:], 1.0, None, ALU.add)
                v.tensor_scalar(ga_v[:], gt1[:], 0.0, None, ALU.is_ge)
                v.tensor_scalar(gb_v[:], gt2[:], 0.0, None, ALU.is_ge)
                v.tensor_scalar(ga_v[:], ga_v[:], 2.0, None, ALU.mult)
                v.tensor_scalar(gb_v[:], gb_v[:], 2.0, None, ALU.mult)
                v.tensor_scalar(ga_v[:], ga_v[:], -1.0, None, ALU.add)
                v.tensor_scalar(gb_v[:], gb_v[:], -1.0, None, ALU.add)
                nc.vector.engine_nop().then_inc(sMODD.h, 1); sMODD.n += 1

            @block.scalar
            def _(s):
                s.wait_ge(sMODD.h, 1)
                s.activation(gt1[:], gt1[:], ACT.Abs)
                s.activation(gt2[:], gt2[:], ACT.Abs).then_inc(sMODD.h, 1); sMODD.n += 1

            @block.vector
            def _(v):
                v.wait_ge(sMODD.h, 2)
                v.tensor_scalar(gt1[:], gt1[:], 0.25, None, ALU.max)
                v.tensor_scalar(gt2[:], gt2[:], 0.25, None, ALU.max)
                v.tensor_tensor(ga_v[:], ga_v[:], gt1[:], ALU.mult)
                v.tensor_tensor(gb_v[:], gb_v[:], gt2[:], ALU.mult).then_inc(sMODD.h, 1); sMODD.n += 1

        if phases == "prep":
            @block.gpsimd
            def _(g):
                g.wait_ge(sAG.h, sAG.n)
            @block.sync
            def _(sync):
                sync.wait_ge(sMODL.h, sMODL.n)
            return nc

        gbuf = sb("gbuf", [128, HCN, NT], BF16)
        gflat = gbuf.ap().rearrange("p h t -> p (h t)")
        stt = gflat.bitcast(F32)[0:1, 19968:22016]
        xsq = gflat[:, 38912:39936].rearrange("p (a t) -> p a t", a=2)
        pA = ps("pA", [128, 2, 512], F32)
        pB = ps("pB", [128, 2, 512], F32)

        prev = {}
        ntiles = 1 if phases == "t0" else TILES
        for T in range(ntiles):
            u = f"t{T}_"
            sSQ = sem(u + "sSQ")
            sSTAT = sem(u + "sSTAT")
            sSS = sem(u + "sSS")
            sBC = sem(u + "sBC")
            sLN = sem(u + "sLN")
            sLNT = sem(u + "sLNT")
            sASTRp = [sem(u + "sASTR0"), sem(u + "sASTR1")]
            sAMM = sem(u + "sAMM")
            sAEP = sem(u + "sAEP")
            sSQ2 = sem(u + "sSQ2")
            sSTAT2 = sem(u + "sSTAT2")
            sSS2 = sem(u + "sSS2")
            sBC2 = sem(u + "sBC2")
            sLN2 = sem(u + "sLN2")
            sLNT2 = sem(u + "sLNT2")
            sWSTRp = [sem(u + "sWSTR0"), sem(u + "sWSTR1")]
            sUV = sem(u + "sUV")
            sSIL = sem(u + "sSIL")
            sGATE = sem(u + "sGATE")
            sW2STRp = [sem(u + "sW2STR0"), sem(u + "sW2STR1")]
            sFMM = sem(u + "sFMM")
            sFEP = sem(u + "sFEP")
            sOTP = sem(u + "sOTP")
            sOEV = sem(u + "sOEV")
            sOSTp = [sem(u + "sOST0"), sem(u + "sOST1")]
            sXST2 = sem(u + "sXST2")
            sXTP2 = sem(u + "sXTP2")
            sXEV2 = sem(u + "sXEV2")

            if T == 1:
                @block.gpsimd
                def _(g, T=T):
                    g.wait_ge(prev["sOSTp"][0].h, prev["sOSTp"][0].n)
                    g.wait_ge(prev["sOSTp"][1].h, prev["sOSTp"][1].n)
                    g.dma_start(
                        out=hx, in_=x_in.rearrange("(T g p) k -> T g p k", T=TILES, g=4, p=128)[T].transpose([1, 0, 2])
                    ).then_inc(sXST2.h, 16)
                sXST2.n += 16

                @block.tensor
                def _(t, T=T):
                    t.wait_ge(sXST2.h, 16)
                    for kc in range(KC):
                        par = kc % 2
                        if kc >= 2:
                            t.wait_ge(sXEV2.h, kc - 1)
                        for tg in range(4):
                            ins = t.transpose(pT.ap()[:, par, tg * 128:(tg + 1) * 128],
                                              hx[:, tg, kc * 128:(kc + 1) * 128], identb[:])
                        ins.then_inc(sXTP2.h, 1)
                sXTP2.n += KC

                @block.scalar
                def _(s, T=T):
                    for kc in range(KC):
                        par = kc % 2
                        s.wait_ge(sXTP2.h, kc + 1)
                        s.copy(xres[:, kc, :], pT.ap()[:, par, 0:512]).then_inc(sXEV2.h, 1)
                sXEV2.n += KC
                sXEVcur, sXEVn = sXEV2, KC
            else:
                sXEVcur, sXEVn = sXEV, KC

            def emit_ln(sSQ_, sSTAT_, sSS_, sBC_, sLN_, sLNT_, scale_v, bias_v,
                        wait_chunk=None, xev=None, xevn=0):
                @block.scalar
                def _(s):
                    for c in range(KC):
                        par = c % 2
                        if xev is not None:
                            s.wait_ge(xev.h, min(c + 1, xevn))
                        if wait_chunk is not None:
                            s.wait_ge(wait_chunk.h, wait_chunk.n - KC + c + 1)
                        if c >= 2:
                            s.wait_ge(sSTAT_.h, c - 1)
                        s.activation(xsq[:, par, :], xres[:, c, :], ACT.Square).then_inc(sSQ_.h, 1)
                sSQ_.n += KC

                @block.tensor
                def _(t):
                    t.wait_ge(sCONST.h, sCONST.n)
                    for c in range(KC):
                        par = c % 2
                        if xev is not None:
                            t.wait_ge(xev.h, min(c + 1, xevn))
                        if wait_chunk is not None:
                            t.wait_ge(wait_chunk.h, wait_chunk.n - KC + c + 1)
                        t.matmul(pS.ap()[0:1, :], stx[:, 0:1], xres[:, c, :], start=(c == 0), stop=(c == KC - 1))
                        t.wait_ge(sSQ_.h, c + 1)
                        ins = t.matmul(pB.ap()[0:1, 0, :], stx[:, 0:1], xsq[:, par, :], start=(c == 0), stop=(c == KC - 1))
                        ins.then_inc(sSTAT_.h, 1)
                sSTAT_.n += KC

                if phases == "stats":
                    @block.gpsimd
                    def _(g):
                        g.wait_ge(sSTAT_.h, KC)
                    return

                @block.vector
                def _(v):
                    v.wait_ge(sSTAT_.h, KC)
                    rstd = stt[:, 0:NT]; nmr = stt[:, NT:2 * NT]
                    mu = stt[:, 2 * NT:3 * NT]; tmp = stt[:, 3 * NT:4 * NT]
                    v.tensor_scalar(mu, pS.ap()[0:1, :], 1.0 / D, None, ALU.mult)
                    v.tensor_scalar(tmp, pB.ap()[0:1, 0, :], 1.0 / D, None, ALU.mult)
                    v.tensor_tensor(rstd, mu, mu, ALU.mult)
                    v.tensor_tensor(tmp, tmp, rstd, ALU.subtract)
                    nc.vector.engine_nop().then_inc(sSS_.h, 1)
                sSS_.n += 1

                @block.scalar
                def _(s):
                    s.wait_ge(sSS_.h, 1)
                    s.activation(stt[:, 3 * NT:4 * NT], stt[:, 3 * NT:4 * NT], ACT.Sqrt, bias=np.float32(1e-5)).then_inc(sSS_.h, 1)
                sSS_.n += 1

                @block.vector
                def _(v):
                    v.wait_ge(sSS_.h, 2)
                    rstd = stt[:, 0:NT]; nmr = stt[:, NT:2 * NT]
                    mu = stt[:, 2 * NT:3 * NT]; tmp = stt[:, 3 * NT:4 * NT]
                    v.reciprocal(rstd, tmp)
                    v.scalar_tensor_tensor(nmr, mu, -1.0, rstd, ALU.mult, ALU.mult)
                    nc.vector.engine_nop().then_inc(sSS_.h, 1)
                sSS_.n += 1

                @block.tensor
                def _(t):
                    t.wait_ge(sSS_.h, 3)
                    t.matmul(pB.ap()[:, 0, :], ones1[:], stt[:, 0:NT], start=True, stop=True)
                    t.matmul(pB.ap()[:, 1, :], ones1[:], stt[:, NT:2 * NT], start=True, stop=True).then_inc(sBC_.h, 1)
                sBC_.n += 1

                if phases == "bcast":
                    @block.gpsimd
                    def _(g):
                        g.wait_ge(sBC_.h, 1)
                    return

                @block.vector
                def _(v):
                    v.wait_ge(sBC_.h, 1)
                    v.wait_ge(sMODD.h, 3)
                    for c in range(KC):
                        par = c % 2
                        if c >= 2:
                            v.wait_ge(sLN_.h, c - 1)
                        v.tensor_tensor(tbuf[:, par, :], xres[:, c, :], pB.ap()[:, 0, :], ALU.mult)
                        v.tensor_tensor(tbuf[:, par, :], tbuf[:, par, :], pB.ap()[:, 1, :], ALU.add).then_inc(sLNT_.h, 1)
                sLNT_.n += KC

                @block.scalar
                def _(s):
                    for c in range(KC):
                        par = c % 2
                        s.wait_ge(sLNT_.h, c + 1)
                        s.activation(hbuf[:, c, :], tbuf[:, par, :], ACT.Identity,
                                     scale=scale_v[:, c:c + 1], bias=bias_v[:, c:c + 1]).then_inc(sLN_.h, 1)
                sLN_.n += KC

            if debug and T == 0:
                sDBG = sem(u + "sDBG")

                @block.sync
                def _(sync):
                    sync.wait_ge(sXEV.h, KC)
                    sync.dma_start(out=dbg_xres, in_=xres[:]).then_inc(sDBG.h, 16)
                sDBG.n += 16

            emit_ln(sSQ, sSTAT, sSS, sBC, sLN, sLNT, ca1, sa_v,
                    wait_chunk=None, xev=sXEVcur, xevn=sXEVn)

            if debug and T == 0:
                @block.sync
                def _(sync):
                    sync.wait_ge(sLN.h, KC)
                    sync.dma_start(out=dbg_h1, in_=hbuf[:]).then_inc(sDBG.h, 16)
                sDBG.n += 16

            if phases in ("stats", "bcast"):
                break

            if phases == "ln1":
                @block.gpsimd
                def _(g):
                    g.wait_ge(sLN.h, KC)
                break

            @block.sync
            def _(sync):
                for NCc in range(KC):
                    b = NCc % 2
                    sync.wait_ge(sAG.h, 3)
                    if NCc >= 2:
                        sync.wait_ge(sAMM.h, NCc - 1)
                    if T == 1 and NCc < 2:
                        sync.wait_ge(prev["sFMM"].h, prev["sFMM"].n)
                    r, ncl = NCc // 4, NCc % 4
                    sync.dma_start(out=awv(b), in_=ag_aw_o.ap()[r, ncl]).then_inc(sASTRp[b].h, 16)
            sASTRp[0].n += 16 * (KC // 2); sASTRp[1].n += 16 * (KC // 2)

            @block.tensor
            def _(t):
                for NCc in range(KC):
                    b = NCc % 2
                    t.wait_ge(sASTRp[b].h, 16 * (NCc // 2 + 1))
                    t.wait_ge(sLN.h, KC)
                    if NCc >= 2:
                        t.wait_ge(sAEP.h, NCc - 1)
                    for k in range(KC):
                        ins = t.matmul(pA.ap()[:, b, :], awv(b)[:, k, :], hbuf[:, k, :],
                                       start=(k == 0), stop=(k == KC - 1))
                    ins.then_inc(sAMM.h, 1)
            sAMM.n += KC

            @block.vector
            def _(v):
                for NCc in range(KC):
                    b = NCc % 2
                    v.wait_ge(sAMM.h, NCc + 1)
                    if debug and T == 0 and NCc == 0:
                        v.wait_ge(sDBG.h, 32)
                    v.scalar_tensor_tensor(xres[:, NCc, :], pA.ap()[:, b, :], ga_v[:, NCc:NCc + 1],
                                           xres[:, NCc, :], ALU.mult, ALU.add).then_inc(sAEP.h, 1)
            sAEP.n += KC

            if phases == "attn":
                @block.gpsimd
                def _(g):
                    g.wait_ge(sAEP.h, KC)
                break

            if debug and T == 0:
                @block.sync
                def _(sync):
                    sync.wait_ge(sAEP.h, KC)
                    sync.dma_start(out=dbg_x2, in_=xres[:]).then_inc(sDBG.h, 16)
                sDBG.n += 16

            emit_ln(sSQ2, sSTAT2, sSS2, sBC2, sLN2, sLNT2, cb1, sb_v,
                    wait_chunk=sAEP, xev=None, xevn=0)

            if debug and T == 0:
                @block.sync
                def _(sync):
                    sync.wait_ge(sLN2.h, KC)
                    sync.dma_start(out=dbg_h2, in_=hbuf[:]).then_inc(sDBG.h, 16)
                sDBG.n += 16

            if phases == "ln2":
                @block.gpsimd
                def _(g):
                    g.wait_ge(sLN2.h, KC)
                break

            @block.sync
            def _(sync):
                for i, (hc, pi) in enumerate(SLOT_ORDER):
                    b = i % 2
                    sync.wait_ge(sAG.h, 10)
                    if i >= 2:
                        sync.wait_ge(sUV.h, 2 * (i - 2) + 2)
                    r, s = HC2RS[hc]
                    pair = PAIRS[pi]
                    sub = s - pair[0]
                    sync.dma_start(out=w1v(b), in_=ag_w13_o[pi].ap()[r, sub, 0]).then_inc(sWSTRp[b].h, 16)
                    sync.dma_start(out=w3v(b), in_=ag_w13_o[pi].ap()[r, sub, 1]).then_inc(sWSTRp[b].h, 16)
            sWSTRp[0].n += 32 * (HCN // 2 + HCN % 2); sWSTRp[1].n += 32 * (HCN // 2)

            @block.tensor
            def _(t):
                for i, (hc, pi) in enumerate(SLOT_ORDER):
                    b = i % 2
                    t.wait_ge(sWSTRp[b].h, 32 * (i // 2 + 1))
                    t.wait_ge(sLN2.h, KC)
                    if i >= 2:
                        t.wait_ge(sSIL.h, i - 1)
                        t.wait_ge(sGATE.h, i - 1)
                    for k in range(KC):
                        ins = t.matmul(pA.ap()[:, b, :], w1v(b)[:, k, :], hbuf[:, k, :],
                                       start=(k == 0), stop=(k == KC - 1))
                    ins.then_inc(sUV.h, 1)
                    for k in range(KC):
                        ins = t.matmul(pB.ap()[:, b, :], w3v(b)[:, k, :], hbuf[:, k, :],
                                       start=(k == 0), stop=(k == KC - 1))
                    ins.then_inc(sUV.h, 1)
            sUV.n += 2 * HCN

            @block.scalar
            def _(s):
                for i, (hc, pi) in enumerate(SLOT_ORDER):
                    b = i % 2
                    s.wait_ge(sUV.h, 2 * i + 1)
                    if i >= 2:
                        s.wait_ge(sGATE.h, i - 1)
                    s.activation(tbuf[:, b, :], pA.ap()[:, b, :], ACT.Silu).then_inc(sSIL.h, 1)
            sSIL.n += HCN

            @block.vector
            def _(v):
                for i, (hc, pi) in enumerate(SLOT_ORDER):
                    b = i % 2
                    v.wait_ge(sUV.h, 2 * i + 2)
                    v.wait_ge(sSIL.h, i + 1)
                    v.tensor_tensor(gbuf[:, hc, :], tbuf[:, b, :], pB.ap()[:, b, :], ALU.mult).then_inc(sGATE.h, 1)
            sGATE.n += HCN

            if phases == "uv":
                @block.gpsimd
                def _(g):
                    g.wait_ge(sGATE.h, HCN)
                break

            if debug and T == 0:
                @block.sync
                def _(sync):
                    sync.wait_ge(sGATE.h, HCN)
                    sync.dma_start(out=dbg_g, in_=gbuf[:, 0:8, :]).then_inc(sDBG.h, 16)
                sDBG.n += 16

            outr = out.rearrange("(Tt g p) d -> Tt g p d", Tt=TILES, g=4, p=128)

            def emit_out_store(sync, DS):
                pd = DS % 2
                sync.wait_ge(sOEV.h, 4 * (DS + 1))
                for tg in range(4):
                    sync.dma_start(out=outr[T, tg][:, DS * 512:(DS + 1) * 512],
                                   in_=ost[:, pd, tg, :, :]).then_inc(sOSTp[pd].h, 16)

            @block.sync
            def _(sync):
                for DC in range(KC):
                    b = DC % 2
                    sync.wait_ge(sAG.h, 12)
                    if DC == 0:
                        sync.wait_ge(sUV.h, 2 * HCN)
                    if DC >= 2:
                        sync.wait_ge(sFMM.h, DC - 1)
                    r, dcl = DC // 4, DC % 4
                    sync.dma_start(out=w2v(b), in_=ag_w2_o.ap()[r, dcl]).then_inc(sW2STRp[b].h, 16)
                    if DC >= 7 and (DC - 7) % 4 == 0:
                        emit_out_store(sync, (DC - 7) // 4)
                emit_out_store(sync, 7)
            sW2STRp[0].n += 16 * (KC // 2); sW2STRp[1].n += 16 * (KC // 2)
            sOSTp[0].n += 16 * 4 * 4; sOSTp[1].n += 16 * 4 * 4

            def emit_f_tp(t, DC):
                b = DC % 2
                t.wait_ge(sFEP.h, DC + 1)
                if DC >= 2:
                    t.wait_ge(sOEV.h, DC - 1)
                for tg in range(4):
                    ins = t.transpose(pB.ap()[:, b, tg * 128:(tg + 1) * 128],
                                      tbuf[:, b, tg * 128:(tg + 1) * 128], identf[:])
                ins.then_inc(sOTP.h, 1)

            @block.tensor
            def _(t):
                for DC in range(KC):
                    b = DC % 2
                    t.wait_ge(sW2STRp[b].h, 16 * (DC // 2 + 1))
                    if DC == 0:
                        t.wait_ge(sGATE.h, HCN)
                    if DC >= 2:
                        t.wait_ge(sFEP.h, DC - 1)
                    for k in range(HCN):
                        ins = t.matmul(pA.ap()[:, b, :], w2v(b)[:, k, :], gbuf[:, k, :],
                                       start=(k == 0), stop=(k == HCN - 1))
                    ins.then_inc(sFMM.h, 1)
                    if DC >= 1:
                        emit_f_tp(t, DC - 1)
                emit_f_tp(t, KC - 1)
            sFMM.n += KC
            sOTP.n += KC

            @block.vector
            def _(v):
                for DC in range(KC):
                    b = DC % 2
                    v.wait_ge(sFMM.h, DC + 1)
                    if DC >= 2:
                        v.wait_ge(sOTP.h, DC - 1)
                    v.scalar_tensor_tensor(tbuf[:, b, :], pA.ap()[:, b, :], gb_v[:, DC:DC + 1],
                                           xres[:, DC, :], ALU.mult, ALU.add).then_inc(sFEP.h, 1)
            sFEP.n += KC

            @block.scalar
            def _(s):
                for DC in range(KC):
                    b = DC % 2
                    pd = (DC // 4) % 2
                    if debug and T == 0 and DC == 0:
                        s.wait_ge(sDBG.h, sDBG.n)
                    s.wait_ge(sOTP.h, DC + 1)
                    if DC >= 8:
                        s.wait_ge(sOSTp[(DC // 4) % 2].h, 64 * (DC // 8))
                    s.copy(ost[:, pd, :, DC % 4, :],
                           pB.ap()[:, b, :].rearrange("p (g n) -> p g n", g=4)).then_inc(sOEV.h, 1)
            sOEV.n += KC


            prev = dict(sOSTp=sOSTp, sFMM=sFMM)

        if "sOSTp" in prev:
            @block.gpsimd
            def _(g):
                g.wait_ge(prev["sOSTp"][0].h, prev["sOSTp"][0].n)
                g.wait_ge(prev["sOSTp"][1].h, prev["sOSTp"][1].n)

    return nc


def kernel(L0_attn_data, L0_attn_weight, L0_attn_mask, L0_ffn_w1, L0_ffn_s1,
           L0_ffn_w2, L0_ffn_s2, L0_ffn_w3, L0_ffn_s3, L0_c, L0_wmod):
    x = np.ascontiguousarray(np.asarray(L0_attn_data, np.float32).reshape(8192, D))
    aw = np.asarray(L0_attn_weight, np.float32)
    am = np.asarray(L0_attn_mask, np.int32)
    w1 = np.asarray(L0_ffn_w1, np.int32)
    s1 = np.asarray(L0_ffn_s1, np.float32)
    w2 = np.asarray(L0_ffn_w2, np.int32)
    s2 = np.asarray(L0_ffn_s2, np.float32)
    w3 = np.asarray(L0_ffn_w3, np.int32)
    s3 = np.asarray(L0_ffn_s3, np.float32)
    cv = np.asarray(L0_c, np.float32).reshape(1, D)
    wm = np.asarray(L0_wmod, np.float32)

    nc = build_nc()
    in_maps = []
    for c in range(NCORES):
        im = {}
        im["x"] = np.ascontiguousarray(x[c * 1024:(c + 1) * 1024])
        im["aw"] = np.ascontiguousarray(aw[c * 512:(c + 1) * 512])
        im["am"] = np.ascontiguousarray(am[c * 512:(c + 1) * 512])
        r0 = STARTS[c] * 4096
        r1 = (STARTS[c] + CNTS[c]) * 4096
        pad = (SLOTS - CNTS[c]) * 4096
        im["w1p"] = np.ascontiguousarray(np.pad(w1[r0:r1], ((0, pad), (0, 0))))
        im["w3p"] = np.ascontiguousarray(np.pad(w3[r0:r1], ((0, pad), (0, 0))))
        im["s1p"] = np.ascontiguousarray(np.pad(s1[2 * r0:2 * r1], (0, 2 * pad)))
        im["s3p"] = np.ascontiguousarray(np.pad(s3[2 * r0:2 * r1], (0, 2 * pad)))
        im["w2p"] = np.ascontiguousarray(w2[c * 44032:(c + 1) * 44032])
        im["s2p"] = np.ascontiguousarray(s2[c * 88064:(c + 1) * 88064])
        im["cvec"] = cv
        im["wmod"] = np.ascontiguousarray(wm[:, c * 3072:(c + 1) * 3072])
        in_maps.append(im)

    res = run_bass_kernel_spmd(nc, in_maps, core_ids=list(range(NCORES)))
    full = np.concatenate([np.asarray(r["out"], np.float32) for r in res.results], axis=0)
    return full.reshape(1, 8192, D)
